# version 36
# baseline (speedup 1.0000x reference)
import contextlib
import os

import numpy as np
import ml_dtypes
import concourse.bass as bass
import concourse.mybir as mybir
from concourse.alu_op_type import AluOpType
from concourse.bass_utils import run_bass_kernel_spmd

F32 = mybir.dt.float32
BF16 = mybir.dt.bfloat16
I32 = mybir.dt.int32
AF = mybir.ActivationFunctionType
AX = mybir.AxisListType

D = 2048
T = 2048
TO = 1024
H = 8
HID = 5632
HC = 44
NB = 4
HPB = 11
NWD = 11
SQ = 0.08838834764831845
EPS = 1e-5
BF = ml_dtypes.bfloat16


class Ctr:
    def __init__(self, h, step):
        self.h, self.step, self.n = h, step, 0

    def reserve(self):
        self.n += self.step
        return self.n


class Plan:

    def __init__(self, nc):
        self.nc = nc
        self.segments = [{k: [] for k in ("tensor", "scalar", "vector", "sync", "gpsimd")}]
        self.awaited = {}

    def em(self, eng, fn, waits=(), inc=None, inc_times=1):
        waits = tuple((c, v) for (c, v) in waits if v > 0)
        for ctr, val in waits:
            assert val <= ctr.n, f"forward wait: {val} > {ctr.n}"
            self.awaited.setdefault(id(ctr), set()).add(val)
        tgt = None
        if inc is not None:
            for _ in range(inc_times):
                tgt = inc.reserve()
        self.segments[-1][eng].append((fn, waits, inc, tgt))
        return tgt

    def flush(self, tag=None, final=False):
        skip = os.environ.get("KNOFLUSH", "").split(",")
        if final:
            self.emit_all()
        elif tag not in skip:
            self.segments.append(
                {k: [] for k in ("tensor", "scalar", "vector", "sync", "gpsimd")})

    def note_awaited(self, ctr, val):
        if val > 0:
            self.awaited.setdefault(id(ctr), set()).add(val)

    def emit_all(self):
        last_inc = {}

        waited_max = {}

        def runner_for(eng, lst):
            def runner(e, lst=lst, eng=eng):
                wm = waited_max.setdefault(eng, {})
                for fn, waits, inc, tgt in lst:
                    for ctr, val in waits:
                        if wm.get(id(ctr), -1) >= val:
                            continue
                        wm[id(ctr)] = val
                        e.wait_ge(ctr.h, val)
                    ins = fn(e)
                    if inc is None:
                        continue
                    if ins is None:
                        last_inc[id(inc)] = tgt
                        continue
                    delta = tgt - last_inc.get(id(inc), 0)
                    if (inc.step == 16 or delta >= 64
                            or tgt in self.awaited.get(id(inc), ())):
                        last_inc[id(inc)] = tgt
                        ins.then_inc(inc.h, delta)
            return runner
        for seg in self.segments:
            st = {k: v for k, v in seg.items() if v}
            if not st:
                continue
            with self.nc.Block() as blk:
                for eng, lst in st.items():
                    getattr(blk, eng)(runner_for(eng, lst))
        self.segments = [{k: [] for k in ("tensor", "scalar", "vector", "sync", "gpsimd")}]


DBG = bool(os.environ.get("KDBG"))


def build_nc():
    nc = bass.Bass("TRN2", target_bir_lowering=False)
    dbg_tensors = {}


    xb = nc.dram_tensor("xb", [T, D], F32, kind="ExternalInput")
    xo = nc.dram_tensor("xo", [TO, D], F32, kind="ExternalInput")
    wq = nc.dram_tensor("wq", [8, 16, 128, 128], F32, kind="ExternalInput")
    wk = nc.dram_tensor("wk", [8, 16, 128, 128], F32, kind="ExternalInput")
    wv = nc.dram_tensor("wv", [16, 128, 1024], F32, kind="ExternalInput")
    bq = nc.dram_tensor("bq", [128, 8], F32, kind="ExternalInput")
    bk = nc.dram_tensor("bk", [128, 8], F32, kind="ExternalInput")
    bvb = nc.dram_tensor("bvb", [128, 1024], F32, kind="ExternalInput")
    wg = nc.dram_tensor("wg", [HC, 16, 128, 128], BF16, kind="ExternalInput")
    wu = nc.dram_tensor("wu", [HC, 16, 128, 128], BF16, kind="ExternalInput")
    wd = nc.dram_tensor("wd", [HC, 128, D], BF16, kind="ExternalInput")
    sg = nc.dram_tensor("sg", [128, HC], F32, kind="ExternalInput")
    su = nc.dram_tensor("su", [128, HC], F32, kind="ExternalInput")
    sdb = nc.dram_tensor("sdb", [128, D], F32, kind="ExternalInput")
    ident = nc.dram_tensor("ident", [128, 128], BF16, kind="ExternalInput")
    out = nc.dram_tensor("out", [TO, D], F32, kind="ExternalOutput")

    cc_inb = nc.dram_tensor("cc_inb", [4, TO, 256], BF16, kind="Internal")
    cc_outb = nc.dram_tensor("cc_outb", [4, 2 * TO, 256], BF16, kind="Internal")
    x2d = nc.dram_tensor("x2d", [TO, D], F32, kind="Internal")
    atmd = nc.dram_tensor("atmd", [TO, 1024], BF16, kind="Internal")

    p = Plan(nc)

    def tap(name, sb_ap, shape, dtype, waits):
        if not DBG:
            return
        dt_ = nc.dram_tensor("dbg_" + name, shape, dtype, kind="ExternalOutput")
        dbg_tensors[name] = dt_
        p.em("sync", lambda e: e.dma_start(out=dt_.ap(), in_=sb_ap), waits=waits, inc=ls)

    ctx = contextlib.ExitStack()
    E = ctx.enter_context

    ls = Ctr(E(nc.semaphore("ls")), 16)
    lg = Ctr(E(nc.semaphore("lg")), 16)
    mm = Ctr(E(nc.semaphore("mm")), 1)
    ac = Ctr(E(nc.semaphore("ac")), 1)
    ve = Ctr(E(nc.semaphore("ve")), 1)
    cc = Ctr(E(nc.semaphore("cc")), 1)
    lw = Ctr(E(nc.semaphore("lw")), 16)
    pl = Ctr(E(nc.semaphore("pl")), 1)

    pt = [E(nc.psum_tensor(f"pt{i}", [128, 512], F32)) for i in range(2)]
    pA = E(nc.psum_tensor("pA", [128, 1024], F32))
    pB = E(nc.psum_tensor("pB", [128, 1024], F32))
    pav = E(nc.psum_tensor("pav", [128, 512], F32))
    psm = E(nc.psum_tensor("psm", [128, 512], F32))

    def qbank(qq):
        t_ = pA if qq < 2 else pB
        return t_[:, (qq % 2) * 512:(qq % 2 + 1) * 512]

    sid = E(nc.sbuf_tensor("sid", [128, 128], BF16))
    sonescol = E(nc.sbuf_tensor("sonescol", [128, 1], BF16))
    sones128 = E(nc.sbuf_tensor("sones128", [128, 128], BF16))
    sonesrow = E(nc.sbuf_tensor("sonesrow", [1, 128], BF16))
    sbq = E(nc.sbuf_tensor("sbq", [128, 8], F32))
    sbk = E(nc.sbuf_tensor("sbk", [128, 8], F32))
    sbqs = E(nc.sbuf_tensor("sbqs", [128, 8], F32))
    ssg = E(nc.sbuf_tensor("ssg", [128, HC], F32))
    ssu = E(nc.sbuf_tensor("ssu", [128, HC], F32))
    stx = E(nc.sbuf_tensor("stx", [128, 32], F32))
    epsc = E(nc.sbuf_tensor("epsc", [128, 1], F32))
    junkb = E(nc.sbuf_tensor("junkb", [128, D], BF16))
    negc = E(nc.sbuf_tensor("negc", [128, H], F32))
    maxs = E(nc.sbuf_tensor("maxs", [128, 1], F32))
    maxb = E(nc.sbuf_tensor("maxb", [128, 1], BF16))
    nc1 = E(nc.sbuf_tensor("nc1", [1, 1], F32))
    nc1b = E(nc.sbuf_tensor("nc1b", [1, 1], BF16))

    p.em("gpsimd", lambda e: e.dma_start(out=sid[:, :], in_=ident.ap()), inc=lg)
    p.em("gpsimd", lambda e: e.dma_start(out=sbq[:, :], in_=bq.ap()), inc=lg)
    p.em("gpsimd", lambda e: e.dma_start(out=sbk[:, :], in_=bk.ap()), inc=lg)
    p.em("gpsimd", lambda e: e.dma_start(out=ssg[:, :], in_=sg.ap()), inc=lg)
    p.em("gpsimd", lambda e: e.dma_start(out=ssu[:, :], in_=su.ap()), inc=lg)
    p.em("gpsimd", lambda e: e.memset(sonescol[:, :], 1.0), inc=lg)
    p.em("gpsimd", lambda e: e.memset(sones128[:, :], 1.0), inc=lg)
    p.em("gpsimd", lambda e: e.memset(sonesrow[:, :], 1.0), inc=lg)
    m_eps = p.em("gpsimd", lambda e: e.memset(epsc[:, :], EPS), inc=lg)
    const_ld = lg.n
    p.em("scalar", lambda e: e.mul(sbqs[:, :], sbq[:, :], SQ),
         waits=[(lg, const_ld)], inc=ac)

    def pipeline(n_tiles, stages, skew, offs=None):
        slots = []
        for tt in range(n_tiles):
            for s in range(len(stages)):
                k = tt * skew + (offs[s] if offs else s)
                slots.append((k, -s, tt, s))
        slots.sort()
        for _, _, tt, s in slots:
            stages[s](tt)

    ln_state = {"prev_norm": [0, 0, 0, 0]}

    def ln_stats_chain(xtile, ld_wait, bank=0, presum=None, mu_ap=None, rstd_ap=None):
        B0 = bank * 8

        def c(i):
            return stx[:, B0 + i:B0 + i + 1]
        muA = c(2) if mu_ap is None else mu_ap
        rsA = c(6) if rstd_ap is None else rstd_ap
        if presum is None:
            v_sum = p.em("vector", lambda e, x=xtile: e.reduce_sum(c(0), x[:, :], AX.X),
                         waits=[ld_wait], inc=ve)
            sum_wait = (ve, v_sum)
        else:
            sum_wait = (ve, presum)
        a_sq = p.em("scalar", lambda e, x=xtile: e.activation(
            junkb[:, :], x[:, :], AF.Square, accum_out=c(1)),
            waits=[ld_wait, (ve, ln_state["prev_norm"][bank])], inc=ac)
        sMu = p.em("scalar", lambda e: e.activation(muA, c(0), AF.Identity, scale=1.0 / D),
                   waits=[sum_wait, (ac, a_sq)], inc=ac)
        sC = p.em("scalar", lambda e: e.activation(c(4), c(0), AF.Square, scale=1.0 / D),
                  waits=[sum_wait, (ac, sMu)], inc=ac)
        sEx = p.em("scalar", lambda e: e.activation(c(3), c(1), AF.Identity,
                                                    scale=1.0 / D, bias=epsc[:, :]),
                   waits=[(ac, sC), (lg, m_eps)], inc=ac)
        sD = p.em("scalar", lambda e: e.activation(c(5), c(4), AF.Identity,
                                                   scale=-1.0, bias=c(3)),
                  waits=[(ac, sEx)], inc=ac)
        vE = p.em("vector", lambda e: e.reciprocal(c(5), c(5)),
                  waits=[(ac, sD)], inc=ve)
        a_rstd = p.em("scalar", lambda e: e.activation(rsA, c(5), AF.Sqrt),
                      waits=[(ve, vE)], inc=ac)
        return a_rstd, muA, rsA, bank

    def ln_normalize(out_ap, in_ap, stats, extra_waits=()):
        a_rstd, mu_ap, rstd_ap, bank = stats
        v_norm = p.em("vector", lambda e: e.tensor_scalar(
            out_ap, in_ap, mu_ap, rstd_ap, AluOpType.subtract, AluOpType.mult),
            waits=[(ac, a_rstd)] + list(extra_waits), inc=ve)
        ln_state["prev_norm"][bank] = v_norm
        return v_norm

    statbuf = E(nc.sbuf_tensor("statbuf", [128, 48], F32))

    scopeB2 = contextlib.ExitStack()
    EB = scopeB2.enter_context
    qT = EB(nc.sbuf_tensor("qT", [128, 8 * T], BF16))
    kT = EB(nc.sbuf_tensor("kT", [128, 8 * T], BF16))
    vsb = EB(nc.sbuf_tensor("vsb", [128, 16 * 1024], BF16))

    scopeA2 = contextlib.ExitStack()
    x1T = scopeA2.enter_context(nc.sbuf_tensor("x1T", [128, 16 * T], BF16))

    scopeB1 = contextlib.ExitStack()
    wqs = [scopeB1.enter_context(nc.sbuf_tensor(f"wqs{i}", [128, 2048], BF16))
           for i in range(2)]

    qk_jobs = [(wq, qT, sbqs, SQ), (wk, kT, sbk, 1.0)]
    qk_evac = {}
    pq_free = [0] * 4
    slab_free = [0, 0]
    slab_ld = {}
    x1T_tile_done = {}
    for i in range(2):
        slab_ld[i] = p.em("gpsimd", lambda e, w=qk_jobs[0][0], oc=i, buf=wqs[i % 2]:
                          e.dma_start(out=buf[:, :], in_=bass.AP(
                              w, oc * 16 * 128 * 128,
                              [[128, 128], [128 * 128, 16], [1, 128]])), inc=lg)

    def emit_qk_chunk(i, qq):
        j, oc = divmod(i, 8)
        wsrc, dstT, biasap, scale = qk_jobs[j]
        buf = wqs[i % 2]
        mml = None
        for dc in range(16):
            w = []
            if dc == 0:
                w = [(lg, slab_ld[i]), (ac, pq_free[qq])]
                w += list(x1T_tile_done[4 * qq + 3])
            mml = p.em("tensor", lambda e, buf=buf, dc=dc, qq=qq, b=qbank(qq):
                       e.matmul(b, buf[:, dc * 128:(dc + 1) * 128],
                                x1T[:, dc * T + qq * 512: dc * T + (qq + 1) * 512],
                                start=(dc == 0), stop=(dc == 15)),
                       waits=w, inc=mm)
        a = p.em("scalar", lambda e, dstT=dstT, oc=oc, qq=qq, b=qbank(qq),
                 biasap=biasap, scale=scale: e.activation(
                     dstT[:, oc * T + qq * 512: oc * T + (qq + 1) * 512],
                     b, AF.Identity, bias=biasap[:, oc:oc + 1], scale=scale),
                 waits=[(mm, mml)], inc=ac)
        pq_free[qq] = a
        qk_evac[(j, oc)] = a

    scopeA1 = contextlib.ExitStack()
    xin = [scopeA1.enter_context(nc.sbuf_tensor(f"xin{i}", [128, D], F32)) for i in range(3)]
    x1bs = [scopeA1.enter_context(nc.sbuf_tensor(f"x1b{i}", [128, D], BF16)) for i in range(2)]

    evac_val = [(ve, 0), (ac, 0)]
    xin_free = [(0, 0)] * 3
    x1b_free_mm = [0, 0]
    p1s = [dict() for _ in range(16)]

    def p1_load(tt):
        buf = xin[tt % 3]
        fa, fv = xin_free[tt % 3]
        eng = "sync" if tt % 2 == 0 else "gpsimd"
        ctr = ls if tt % 2 == 0 else lg
        p1s[tt]["ld"] = (ctr, p.em(eng, lambda e, tt=tt, buf=buf: e.dma_start(
            out=buf[:, :], in_=bass.AP(xb, tt * 128 * D, [[D, 128], [1, D]])),
            waits=[(ac, fa), (ve, fv)], inc=ctr))

    def p1_stats(tt):
        buf = xin[tt % 3]
        p1s[tt]["ln"] = ln_stats_chain(
            buf, p1s[tt]["ld"], bank=tt % 3,
            rstd_ap=statbuf[:, 3 * tt:3 * tt + 1])

    def p1_norm(tt):
        buf = xin[tt % 3]
        stats = p1s[tt]["ln"]
        x1b = x1bs[tt % 2]
        v_norm = ln_normalize(x1b[:, :], buf[:, :], stats,
                              extra_waits=[(mm, x1b_free_mm[tt % 2])])
        xin_free[tt % 3] = (stats[0], v_norm)
        v_nmr = p.em("vector", lambda e, tt=tt, mu=stats[1], rs=stats[2]:
                     e.scalar_tensor_tensor(
                         statbuf[:, 3 * tt + 1:3 * tt + 2], mu, -1.0, rs,
                         AluOpType.mult, AluOpType.mult),
                     waits=[(ac, stats[0]), (ve, v_norm)], inc=ve)
        v_dnm = p.em("vector", lambda e, tt=tt: e.tensor_scalar_mul(
            statbuf[:, 3 * tt + 2:3 * tt + 3], statbuf[:, 3 * tt + 1:3 * tt + 2],
            float(D)), waits=[(ve, v_nmr)], inc=ve)
        ln_state["prev_norm"][tt % 3] = v_dnm
        for g in range(4):
            bank = pt[g % 2]
            mmlast = None
            for j in range(4):
                dc = g * 4 + j
                w = []
                if j == 0:
                    w = [(ve, v_norm), evac_val[g % 2]]
                mmlast = p.em("tensor", lambda e, bank=bank, x1b=x1b, j=j, dc=dc: e.matmul(
                    bank[:, j * 128:(j + 1) * 128], x1b[:, dc * 128:(dc + 1) * 128],
                    sid[:, :], start=True, stop=True), waits=w, inc=mm)
            dst = bass.AP(x1T, (g * 4) * T + tt * 128, [[16 * T, 128], [T, 4], [1, 128]])
            eng = "vector" if g % 2 == 0 else "scalar"
            ctr = ve if g % 2 == 0 else ac
            if g % 2 == 0:
                evac_val[0] = (ve, p.em("vector", lambda e, bank=bank, dst=dst:
                                        e.tensor_copy(dst, bank[:, :]),
                                        waits=[(mm, mmlast)], inc=ve))
            else:
                evac_val[1] = (ac, p.em("scalar", lambda e, bank=bank, dst=dst:
                                        e.copy(dst, bank[:, :]),
                                        waits=[(mm, mmlast)], inc=ac))
        x1b_free_mm[tt % 2] = mm.n
        x1T_tile_done[tt] = (evac_val[0], evac_val[1])
        if tt % 4 == 3:
            qq = tt // 4
            emit_qk_chunk(0, qq)
            emit_qk_chunk(1, qq)
            if tt == 15:
                slab_free[0] = ac.n
                slab_free[1] = ac.n

    pipeline(16, [p1_load, p1_stats, p1_norm], skew=1)

    x1T_ready_ve = ve.n
    x1T_ready_ac = ac.n
    p1_done_mm = mm.n
    tap("x1T", x1T[:, :], [128, 16 * T], BF16, [(ve, ve.n)])
    p1_done_ac = ac.n
    p1_done_ve = ve.n
    p.flush("p1")
    scopeA1.close()

    for i in range(2, 4):
        slab_ld[i] = p.em("gpsimd", lambda e, w=qk_jobs[0][0], oc=i, buf=wqs[i % 2]:
                          e.dma_start(out=buf[:, :], in_=bass.AP(
                              w, oc * 16 * 128 * 128,
                              [[128, 128], [128 * 128, 16], [1, 128]])),
                          waits=[(ac, slab_free[i % 2])], inc=lg)
    last_k_mm = 0
    for j, (wsrc, dstT, biasap, scale) in enumerate(qk_jobs):
        for oc in range(8):
            i = j * 8 + oc
            if i < 2:
                continue
            buf = wqs[i % 2]
            mml = [0] * 4
            for dc in range(16):
                for qq in range(4):
                    w = []
                    if dc == 0:
                        w = [(lg, slab_ld[i]), (ac, pq_free[qq])]
                    mml[qq] = p.em("tensor", lambda e, buf=buf, dc=dc, qq=qq, b=qbank(qq):
                                   e.matmul(b, buf[:, dc * 128:(dc + 1) * 128],
                                            x1T[:, dc * T + qq * 512: dc * T + (qq + 1) * 512],
                                            start=(dc == 0), stop=(dc == 15)),
                                   waits=w, inc=mm)
            last_k_mm = mml[3]
            for qq in range(4):
                a = p.em("scalar", lambda e, dstT=dstT, oc=oc, qq=qq, b=qbank(qq),
                         biasap=biasap, scale=scale: e.activation(
                             dstT[:, oc * T + qq * 512: oc * T + (qq + 1) * 512],
                             b, AF.Identity, bias=biasap[:, oc:oc + 1], scale=scale),
                         waits=[(mm, mml[qq])], inc=ac)
                pq_free[qq] = a
                qk_evac[(j, oc)] = a
            slab_free[i % 2] = ac.n
            if i + 2 < 16:
                j2, oc2 = divmod(i + 2, 8)
                slab_ld[i + 2] = p.em("gpsimd", lambda e, w=qk_jobs[j2][0], oc2=oc2,
                                      buf=wqs[i % 2]: e.dma_start(
                                          out=buf[:, :], in_=bass.AP(
                                              w, oc2 * 16 * 128 * 128,
                                              [[128, 128], [128 * 128, 16], [1, 128]])),
                                      waits=[(ac, slab_free[i % 2])], inc=lg)

    negc_ready = [0] * H
    pp = {"bank_free": [evac_val[0], evac_val[1]], "maxs_ac": 0, "maxb_mm": 0,
          "nc1_ac": 0, "nc1b_mm": 0}
    pp_state = [dict() for _ in range(H)]

    def pp0(h):
        bank = pt[h % 2]
        w = [(ac, qk_evac[(0, h)]), (ac, qk_evac[(1, h)]), pp["bank_free"][h % 2]]
        pp_state[h]["m1"] = p.em("tensor", lambda e, h=h, bank=bank: e.matmul(
            bank[:, :], kT[:, h * T: h * T + 128], qT[:, h * T: h * T + 512],
            start=True, stop=True), waits=w, inc=mm)

    def pp1(h):
        bank = pt[h % 2]
        v1 = p.em("vector", lambda e, bank=bank: e.reduce_max(maxs[:, 0:1], bank[:, :], AX.X),
                  waits=[(mm, pp_state[h]["m1"]), (ac, pp["maxs_ac"])], inc=ve)
        a1 = p.em("scalar", lambda e: e.copy(maxb[:, :], maxs[:, 0:1]),
                  waits=[(ve, v1), (mm, pp["maxb_mm"])], inc=ac)
        pp["maxs_ac"] = a1
        pp_state[h]["a1"] = a1

    def pp2(h):
        bank = pt[h % 2]
        m2 = p.em("tensor", lambda e, bank=bank: e.matmul(
            bank[0:1, 0:128], maxb[:, :], sid[:, :], start=True, stop=True),
            waits=[(ac, pp_state[h]["a1"])], inc=mm)
        pp["maxb_mm"] = m2
        v2 = p.em("vector", lambda e, bank=bank: e.reduce_max(
            nc1[0:1, 0:1], bank[0:1, 0:128], AX.X, negate=True),
            waits=[(mm, m2), (ac, pp["nc1_ac"])], inc=ve)
        a2 = p.em("scalar", lambda e: e.copy(nc1b[:, :], nc1[0:1, 0:1]),
                  waits=[(ve, v2), (mm, pp["nc1b_mm"])], inc=ac)
        pp_state[h]["a2"] = a2
        pp["nc1_ac"] = a2

    def pp3(h):
        bank = pt[h % 2]
        m3 = p.em("tensor", lambda e, bank=bank: e.matmul(
            bank[:, 300:301], sonesrow[:, :], nc1b[:, :], start=True, stop=True),
            waits=[(ac, pp_state[h]["a2"])], inc=mm)
        pp["nc1b_mm"] = m3
        a3 = p.em("scalar", lambda e, h=h, bank=bank: e.copy(negc[:, h:h + 1], bank[:, 300:301]),
                  waits=[(mm, m3)], inc=ac)
        pp["bank_free"][h % 2] = (ac, a3)
        negc_ready[h] = a3

    pp_stages = [pp0, pp1, pp2, pp3]
    pp_slots = {}
    for h in range(H):
        for k in range(4):
            pp_slots.setdefault(2 * h + k, []).append((4 - k, h, k))
    for v in pp_slots.values():
        v.sort()

    def pp_emit_slot(t):
        for _, h, k in pp_slots.get(t, []):
            pp_stages[k](h)

    scopeB1b = contextlib.ExitStack()
    wvf = scopeB1b.enter_context(nc.sbuf_tensor("wvf", [128, 16 * 1024], BF16))
    sbvb = scopeB1b.enter_context(nc.sbuf_tensor("sbvb", [128, 1024], BF16))
    p.em("gpsimd", lambda e: e.dma_start(out=sbvb[:, :], in_=bvb.ap()), inc=lg)
    for dc in range(16):
        p.em("gpsimd", lambda e, dc=dc: e.dma_start(
            out=wvf[:, dc * 1024:(dc + 1) * 1024],
            in_=bass.AP(wv, dc * 128 * 1024, [[1024, 128], [1, 1024]])), inc=lg)
    wv_ld = lg.n
    v_evac_prev = 0
    for tc in range(16):
        mml = [0, 0]
        for dc in range(16):
            for half in range(2):
                w = []
                if dc == 0:
                    w = [(lg, wv_ld), (ve, v_evac_prev), (ac, pq_free[2 + half])]
                mml[half] = p.em("tensor", lambda e, dc=dc, tc=tc, half=half, b=[pav, psm][half]:
                                 e.matmul(b[:, :],
                                          x1T[:, dc * T + tc * 128: dc * T + (tc + 1) * 128],
                                          wvf[:, dc * 1024 + half * 512: dc * 1024 + (half + 1) * 512],
                                          start=(dc == 0), stop=(dc == 15)),
                                 waits=w, inc=mm)
        for half in range(2):
            v_evac_prev = p.em("vector", lambda e, tc=tc, half=half, b=[pav, psm][half]:
                               e.tensor_tensor(
                                   vsb[:, tc * 1024 + half * 512: tc * 1024 + (half + 1) * 512],
                                   b[:, :], sbvb[:, half * 512:(half + 1) * 512],
                                   AluOpType.add),
                               waits=[(mm, mml[half])], inc=ve)
        pp_emit_slot(tc)

    tap("qT", qT[:, :], [128, 8 * T], BF16, [(ac, ac.n)])
    tap("kT", kT[:, :], [128, 8 * T], BF16, [(ac, ac.n)])
    tap("vsb", vsb[:, :], [128, 16 * 1024], BF16, [(ve, ve.n)])
    for t in range(16, 20):
        pp_emit_slot(t)
    p.flush("v")
    scopeB1b.close()
    scopeB1.close()
    scopeA2.close()

    scopeC = contextlib.ExitStack()
    EC = scopeC.enter_context
    expT = EC(nc.sbuf_tensor("expT", [128, 16 * 512], BF16))
    avhs = [EC(nc.sbuf_tensor(f"avh{i}", [128, T], BF16)) for i in range(2)]
    atm_peer = EC(nc.sbuf_tensor("atm_peer", [128, 8 * 1024], BF16))
    atm_own = EC(nc.sbuf_tensor("atm_own", [128, 8 * 1024], BF16))
    sumsrows = [EC(nc.sbuf_tensor(f"sumsrow{i}", [1, T], F32)) for i in range(2)]
    sumhi = EC(nc.sbuf_tensor("sumhi", [1, T], BF16))
    sumlo32 = EC(nc.sbuf_tensor("sumlo32", [1, T], F32))
    sumlo = EC(nc.sbuf_tensor("sumlo", [1, T], BF16))
    recipTs = [EC(nc.sbuf_tensor(f"recipT{i}", [128, 16], F32)) for i in range(2)]

    PAIRS = [pA, pB]
    exp_pair_val = [pq_free[1], pq_free[3]]
    onesval = [0] * 16
    avcopy_prev = v_evac_prev
    avcopy_prev_ac = 0
    sumscopy_prev_ac = 0
    recip_guard = [pp["bank_free"][0], pp["bank_free"][1]]
    pt_norm = [0, 0]
    epi_pending = None

    epi_state = {"sums_tr_mm": 0}

    def emit_epilogue(hp, sums_ac, av_ac):
        par = hp % 2
        sumsrow = sumsrows[par]
        recipT = recipTs[par]
        avh = avhs[par]
        v1 = p.em("vector", lambda e: e.tensor_copy(sumhi[:, :], sumsrow[:, :]),
                  waits=[(ac, sums_ac), (mm, epi_state["sums_tr_mm"])], inc=ve)
        v2 = p.em("vector", lambda e: e.tensor_sub(sumlo32[:, :], sumsrow[:, :], sumhi[:, :]),
                  waits=[(ve, v1)], inc=ve)
        v3 = p.em("vector", lambda e: e.tensor_copy(sumlo[:, :], sumlo32[:, :]),
                  waits=[(ve, v2)], inc=ve)
        mml = 0
        for qt in range(16):
            w = [(ve, v3), recip_guard[0]] if qt == 0 else []
            p.em("tensor", lambda e, qt=qt: e.matmul(
                pt[0][:, qt:qt + 1], sumhi[0:1, qt * 128:(qt + 1) * 128],
                sonesrow[0:1, 0:1], start=True, stop=True), waits=w, inc=mm)
            mml = p.em("tensor", lambda e, qt=qt: e.matmul(
                pt[0][:, qt:qt + 1], sumlo[0:1, qt * 128:(qt + 1) * 128],
                sonesrow[0:1, 0:1], start=False, stop=True), inc=mm)
        epi_state["sums_tr_mm"] = mml
        rv = p.em("vector", lambda e, recipT=recipT: e.reciprocal(
            recipT[:, 0:16], pt[0][:, 0:16]), waits=[(mm, mml)], inc=ve)
        for qt in range(16):
            bank = pt[qt % 2]
            w = [(ve, pt_norm[qt % 2]), (ac, av_ac)]
            if qt == 0:
                w.append((ve, rv))
            if qt == 1:
                w.append(recip_guard[1])
            m1 = p.em("tensor", lambda e, qt=qt, bank=bank, avh=avh: e.matmul(
                bank[:, 0:128], avh[:, qt * 128:(qt + 1) * 128], sid[:, :],
                start=True, stop=True), waits=w, inc=mm)
            dst = (atm_own[:, qt * 1024 + hp * 128: qt * 1024 + (hp + 1) * 128]
                   if qt < 8 else
                   atm_peer[:, (qt - 8) * 1024 + hp * 128: (qt - 8) * 1024 + (hp + 1) * 128])
            pt_norm[qt % 2] = p.em("vector",
                                   lambda e, qt=qt, dst=dst, bank=bank, recipT=recipT:
                                   e.tensor_scalar(
                                       dst,
                                       bank[:, 0:128], recipT[:, qt:qt + 1], None,
                                       AluOpType.mult),
                                   waits=[(mm, m1)], inc=ve)
        recip_guard[1] = (ve, pt_norm[1])
        recip_guard[0] = (ve, pt_norm[0])

    attn_q_ve = [0, 0, 0, 0]
    for h in range(H):
        avh = avhs[h % 2]
        sumsrow = sumsrows[h % 2]
        for qb in range(4):
            scv = [0] * 16
            for mp in range(8):
                mj0 = 2 * mp
                P = mp % 2
                bankT = PAIRS[P]
                for k in range(2):
                    mj = mj0 + k
                    w = []
                    if k == 0:
                        w.append((ac, exp_pair_val[P]))
                        if mj0 == 0:
                            w.append((ac, negc_ready[h]))
                    scv[mj] = p.em("tensor", lambda e, h=h, qb=qb, mj=mj, k=k, bankT=bankT:
                                   e.matmul(
                                       bankT[:, k * 512:(k + 1) * 512],
                                       kT[:, h * T + mj * 128: h * T + (mj + 1) * 128],
                                       qT[:, h * T + qb * 512: h * T + (qb + 1) * 512],
                                       start=True, stop=True), waits=w, inc=mm)
                we = [(mm, scv[mj0 + 1]), (mm, onesval[mj0]), (mm, onesval[mj0 + 1])]
                if h == 0 and qb == 0 and mp == 0:
                    we.append((ve, v_evac_prev))
                exp_pair_val[P] = p.em("scalar", lambda e, mj0=mj0, bankT=bankT, h=h:
                                       e.activation(
                                           expT[:, mj0 * 512:(mj0 + 2) * 512],
                                           bankT[:, :], AF.Exp, bias=negc[:, h:h + 1]),
                                       waits=we, inc=ac)
                if mp >= 1:
                    for mj in (mj0 - 2, mj0 - 1):
                        wa = [(ac, exp_pair_val[(mp - 1) % 2])]
                        if mj == 0:
                            wa += [(ve, avcopy_prev), (ac, avcopy_prev_ac)]
                        p.em("tensor", lambda e, h=h, mj=mj: e.matmul(
                            pav[:, :], vsb[:, mj * 1024 + h * 128: mj * 1024 + (h + 1) * 128],
                            expT[:, mj * 512:(mj + 1) * 512],
                            start=(mj == 0), stop=(mj == 15)), waits=wa, inc=mm)
                        wo = []
                        if mj == 0:
                            wo = [(ac, sumscopy_prev_ac)]
                        onesval[mj] = p.em("tensor", lambda e, mj=mj: e.matmul(
                            psm[:, :], sones128[:, :], expT[:, mj * 512:(mj + 1) * 512],
                            start=(mj == 0), stop=(mj == 15)), waits=wo, inc=mm)
            for mj in (14, 15):
                wa = [(ac, exp_pair_val[1])]
                p.em("tensor", lambda e, h=h, mj=mj: e.matmul(
                    pav[:, :], vsb[:, mj * 1024 + h * 128: mj * 1024 + (h + 1) * 128],
                    expT[:, mj * 512:(mj + 1) * 512],
                    start=(mj == 0), stop=(mj == 15)), waits=wa, inc=mm)
                onesval[mj] = p.em("tensor", lambda e, mj=mj: e.matmul(
                    psm[:, :], sones128[:, :], expT[:, mj * 512:(mj + 1) * 512],
                    start=(mj == 0), stop=(mj == 15)), inc=mm)
            avcopy_prev_ac = p.em("scalar", lambda e, qb=qb, avh=avh: e.copy(
                avh[:, qb * 512:(qb + 1) * 512], pav[:, :]),
                waits=[(mm, onesval[15])], inc=ac)
            sumscopy_prev_ac = p.em("scalar", lambda e, qb=qb, sumsrow=sumsrow: e.copy(
                sumsrow[0:1, qb * 512:(qb + 1) * 512], psm[0:1, :]),
                waits=[(mm, onesval[15])], inc=ac)
            avcopy_prev = 0
            if qb == 0 and epi_pending is not None:
                hp = epi_pending[0]
                emit_epilogue(*epi_pending)
                epi_pending = None
                if hp % 2 == 1:
                    attn_q_ve[hp // 2] = ve.n
        epi_pending = (h, sumscopy_prev_ac, avcopy_prev_ac)
    emit_epilogue(*epi_pending)
    attn_q_ve[3] = ve.n

    attn_done_ve = ve.n
    tap("negc", negc[:, :], [128, H], F32, [(ac, ac.n)])
    tap("atm_own", atm_own[:, :], [128, 8 * 1024], BF16, [(ve, attn_done_ve)])
    tap("atm_peer", atm_peer[:, :], [128, 8 * 1024], BF16, [(ve, attn_done_ve)])

    cc_done = [0, 0, 0, 0]
    st_cc = 0
    for s in range(4):
        st_cc = p.em("gpsimd", lambda e, s=s: e.dma_start(
            out=bass.AP(cc_inb, s * TO * 256, [[256, 128], [128 * 256, 8], [1, 256]]),
            in_=bass.AP(atm_peer, s * 256, [[8 * 1024, 128], [1024, 8], [1, 256]])),
            waits=[(ve, attn_q_ve[s])], inc=lg)
        cc_done[s] = p.em("gpsimd", lambda e, s=s: e.collective_compute(
            "AllGather", AluOpType.bypass,
            replica_groups=[[0, 1], [2, 3], [4, 5], [6, 7]],
            ins=[bass.AP(cc_inb, s * TO * 256, [[256, TO], [1, 256]]).opt()],
            outs=[bass.AP(cc_outb, s * 2 * TO * 256, [[256, 2 * TO], [1, 256]]).opt()]),
            waits=[(lg, st_cc)], inc=cc)
        p.note_awaited(cc, cc_done[s])
    cc_send_done = st_cc
    spill_own = p.em("gpsimd", lambda e: e.dma_start(
        out=bass.AP(atmd, 0, [[1024, 128], [128 * 1024, 8], [1, 1024]]),
        in_=bass.AP(atm_own, 0, [[8 * 1024, 128], [1024, 8], [1, 1024]])),
        waits=[(ve, attn_done_ve)], inc=lg)

    scopeC.close()
    scopeB2.close()

    scopeE2 = contextlib.ExitStack()
    E2 = scopeE2.enter_context
    down = E2(nc.sbuf_tensor("down", [128, 8 * D], F32))
    sguT = E2(nc.sbuf_tensor("sguT", [128, 8], F32))

    sgur = sgur2 = sguh = None
    scopeX3T = contextlib.ExitStack()
    x3T = scopeX3T.enter_context(nc.sbuf_tensor("x3T", [128, 16 * TO], BF16))

    scopeD = contextlib.ExitStack()
    ED = scopeD.enter_context
    afp = [ED(nc.sbuf_tensor(f"afp{i}", [128, 1024], BF16)) for i in range(3)]
    afo = [ED(nc.sbuf_tensor(f"afo{i}", [128, 1024], BF16)) for i in range(3)]
    xol = [ED(nc.sbuf_tensor(f"xol{i}", [128, D], F32)) for i in range(3)]
    tmpas = [ED(nc.sbuf_tensor(f"tmpa{i}", [128, D], F32)) for i in range(4)]
    tmpbs = [ED(nc.sbuf_tensor(f"tmpb{i}", [128, D], F32)) for i in range(2)]
    x3bs = [ED(nc.sbuf_tensor(f"x3b{i}", [128, D], BF16)) for i in range(2)]

    rb_vals = [0] * 8
    afp_free = [0, 0, 0]
    own_ld = [0] * 8
    afo_free = [0, 0, 0]
    mrg_prev = [0, 0, 0, 0]

    def emit_rb(tt):
        def fn(e, tt=tt):
            pid = e.partition_id()
            with e.register(f"off{tt}") as off:
                e.reg_alu(off, pid, 1, AluOpType.bitwise_and)
                e.reg_alu(off, off, 1, AluOpType.bitwise_xor)
                e.reg_mul(off, off, TO * 256)
                e.reg_add(off, off, tt * 128 * 256)
                e.dma_start(
                    out=bass.AP(afp[tt % 3], 0, [[1024, 128], [256, 4], [1, 256]]),
                    in_=bass.AP(cc_outb, off,
                                [[256, 128], [2 * TO * 256, 4], [1, 256]])
                ).then_inc(lg.h, 16)
            return None
        rb_vals[tt] = p.em("gpsimd", fn,
                           waits=[(cc, cc_done[3]), (ac, afp_free[tt % 3])],
                           inc=lg)

    xol_free = [(0, 0)] * 3
    evac_val5 = [(ve, 0), (ac, 0)]
    x3b_free_mm = [0, 0]
    tmpa_free = [[], [], [], []]
    tmpb_free_ve = [0, 0]
    x2_spill = [0] * 8
    x3T_half_w = [None, None]
    p5s = [dict() for _ in range(8)]

    def p5_load(tt):
        par = tt % 3
        buf = xol[par]
        fa, fv = xol_free[par]
        w = [(ac, fa), (ve, fv)]
        if tt < 2:
            w.append((ve, attn_done_ve))
            w.append((lg, cc_send_done))
        p5s[tt]["ld"] = (ls, p.em("sync", lambda e, tt=tt, buf=buf: e.dma_start(
            out=buf[:, :], in_=bass.AP(xo, tt * 128 * D, [[D, 128], [1, D]])),
            waits=w, inc=ls))
        emit_rb(tt)
        own_ld[tt] = p.em("sync", lambda e, tt=tt: e.dma_start(
            out=afo[tt % 3][:, :],
            in_=bass.AP(atmd, tt * 128 * 1024, [[1024, 128], [1, 1024]])),
            waits=[(lg, spill_own), (ac, afo_free[tt % 3])], inc=ls)

    def p5_attio(tt):
        par = tt % 4
        tmpa = tmpas[par]
        nmr = statbuf[:, 3 * tt + 1:3 * tt + 2]
        dnm = statbuf[:, 3 * tt + 2:3 * tt + 3]
        sw = [(ls, own_ld[tt]), (ve, mrg_prev[par])] + tmpa_free[par]
        if tt < 3:
            sw.append((ve, attn_done_ve))
            sw.append((lg, cc_send_done))
        s_own = p.em("scalar", lambda e, tt=tt, tmpa=tmpa, nmr=nmr, par=par: e.activation(
            tmpa[:, 0:1024], afo[tt % 3][:, :], AF.Identity, bias=nmr,
            accum_out=stx[:, par * 8:par * 8 + 1]),
            waits=sw, inc=ac)
        afo_free[tt % 3] = s_own
        s_peer = p.em("scalar", lambda e, tt=tt, tmpa=tmpa, nmr=nmr, par=par: e.activation(
            tmpa[:, 1024:2048], afp[tt % 3][:, :], AF.Identity, bias=nmr,
            accum_out=stx[:, par * 8 + 7:par * 8 + 8]),
            waits=[(lg, rb_vals[tt]), (ac, s_own)], inc=ac)
        afp_free[tt % 3] = s_peer
        p5s[tt]["halves"] = s_peer
        v_mrg = p.em("vector", lambda e, tt=tt, dnm=dnm, par=par: e.scalar_tensor_tensor(
            stx[:, par * 8:par * 8 + 1],
            stx[:, par * 8:par * 8 + 1], dnm,
            stx[:, par * 8 + 7:par * 8 + 8],
            AluOpType.subtract, AluOpType.add),
            waits=[(ac, s_peer)], inc=ve)
        mrg_prev[par] = v_mrg
        p5s[tt]["attio"] = v_mrg

    def p5_fin(tt):
        buf, tmpa = xol[tt % 3], tmpas[tt % 4]
        v_fin = p.em("vector", lambda e, buf=buf, tmpa=tmpa, tt=tt:
                     e.scalar_tensor_tensor(
                         tmpa[:, :], buf[:, :], statbuf[:, 3 * tt:3 * tt + 1],
                         tmpa[:, :], AluOpType.mult, AluOpType.add),
                     waits=[p5s[tt]["ld"], (ac, p5s[tt]["halves"])], inc=ve)
        p5s[tt]["fin"] = v_fin

    def p5_ln2(tt):
        p5s[tt]["ln2"] = ln_stats_chain(tmpas[tt % 4], (ve, p5s[tt]["fin"]),
                                        bank=tt % 4, presum=p5s[tt]["attio"])

    def p5_x2a(tt):
        tmpa, tmpb = tmpas[tt % 4], tmpbs[tt % 2]
        p5s[tt]["ln2n"] = ln_normalize(tmpb[:, :], tmpa[:, :], p5s[tt]["ln2"],
                                       extra_waits=[(ve, tmpb_free_ve[tt % 2])])

    def p5_x2b(tt):
        par = tt % 4
        buf, tmpa, tmpb = xol[tt % 3], tmpas[par], tmpbs[tt % 2]
        v_x2 = p.em("vector", lambda e, buf=buf, tmpa=tmpa, tmpb=tmpb, par=par:
                    e.scalar_tensor_tensor(
                        tmpa[:, :], buf[:, :], 0.0, tmpb[:, :],
                        AluOpType.add, AluOpType.add,
                        accum_out=stx[:, par * 8:par * 8 + 1]),
                    waits=[(ve, p5s[tt]["ln2n"])], inc=ve)
        tmpb_free_ve[tt % 2] = v_x2
        p5s[tt]["x2"] = v_x2
        xol_free[tt % 3] = (0, v_x2)
        x2_spill[tt] = p.em("sync", lambda e, tt=tt, tmpa=tmpa: e.dma_start(
            out=bass.AP(x2d, tt * 128 * D, [[D, 128], [1, D]]), in_=tmpa[:, :]),
            waits=[(ve, v_x2)], inc=ls)

    def p5_ln3(tt):
        p5s[tt]["ln3"] = ln_stats_chain(tmpas[tt % 4], (ve, p5s[tt]["x2"]),
                                        bank=tt % 4, presum=p5s[tt]["x2"])

    def p5_x3(tt):
        par = tt % 4
        tmpa, x3b = tmpas[par], x3bs[tt % 2]
        v_x3 = ln_normalize(x3b[:, :], tmpa[:, :], p5s[tt]["ln3"],
                            extra_waits=[(mm, x3b_free_mm[tt % 2])])
        tmpa_free[par] = [(ve, v_x3), (ls, x2_spill[tt])]
        for g in range(4):
            bank = pt[g % 2]
            mmlast = None
            for j in range(4):
                dc = g * 4 + j
                w = []
                if j == 0:
                    w = [(ve, v_x3), evac_val5[g % 2]]
                    if tt == 0 and g < 2:
                        w.append((ve, pt_norm[g]))
                mmlast = p.em("tensor", lambda e, bank=bank, x3b=x3b, j=j, dc=dc: e.matmul(
                    bank[:, j * 128:(j + 1) * 128], x3b[:, dc * 128:(dc + 1) * 128],
                    sid[:, :], start=True, stop=True), waits=w, inc=mm)
            dst = bass.AP(x3T, (g * 4) * TO + tt * 128, [[16 * TO, 128], [TO, 4], [1, 128]])
            if g % 2 == 0:
                evac_val5[0] = (ve, p.em("vector", lambda e, bank=bank, dst=dst:
                                         e.tensor_copy(dst, bank[:, :]),
                                         waits=[(mm, mmlast)], inc=ve))
            else:
                evac_val5[1] = (ac, p.em("scalar", lambda e, bank=bank, dst=dst:
                                         e.copy(dst, bank[:, :]),
                                         waits=[(mm, mmlast)], inc=ac))
        x3b_free_mm[tt % 2] = mm.n
        if tt == 3:
            x3T_half_w[0] = [(ve, ve.n), (ac, ac.n)]

    pipeline(8, [p5_load, p5_attio, p5_fin, p5_ln2, p5_x2a, p5_x2b, p5_ln3, p5_x3],
             skew=2, offs=[0, 1, 2, 3, 5, 6, 7, 8])
    x3T_half_w[1] = [(ve, ve.n), (ac, ac.n)]

    x3T_ready_w = [(ve, ve.n), (ac, ac.n)]
    p.flush("p5")
    scopeD.close()

    scopeE1 = contextlib.ExitStack()
    E1 = scopeE1.enter_context
    wgs = [E1(nc.sbuf_tensor(f"wgs{i}", [128, 2048], BF16)) for i in range(2)]
    wus = [E1(nc.sbuf_tensor(f"wus{i}", [128, 2048], BF16)) for i in range(2)]
    wds = [E1(nc.sbuf_tensor(f"wds{i}", [128, D], BF16)) for i in range(NWD)]
    gu = E1(nc.sbuf_tensor("gu", [128, HPB * TO], BF16))
    gtmp = E1(nc.sbuf_tensor("gtmp", [128, 1024], BF16))
    sxr = E1(nc.sbuf_tensor("sxr", [1, TO], F32))
    sxt = E1(nc.sbuf_tensor("sxt", [1, TO], F32))
    sxlo1 = E1(nc.sbuf_tensor("sxlo1", [1, TO], BF16))
    sxh = E1(nc.sbuf_tensor("sxh", [2, TO], BF16))
    ones2 = E1(nc.sbuf_tensor("ones2", [2, 512], BF16))
    sgur, sgur2, sguh = sxr, sxt, sxh

    p.em("gpsimd", lambda e: e.memset(ones2[:, :], 1.0), inc=lg)
    misc6_ld = lg.n

    sxmm = [0, 0]
    for half in range(2):
        for dc in range(16):
            sxmm[half] = p.em("tensor", lambda e, dc=dc, half=half: e.matmul(
                qbank(half), sones128[:, :],
                x3T[:, dc * TO + half * 512: dc * TO + (half + 1) * 512],
                start=(dc == 0), stop=(dc == 15)),
                waits=list(x3T_ready_w) if (dc == 0) else [], inc=mm)
    a_sx0 = p.em("scalar", lambda e: e.copy(sxr[0:1, 0:512], qbank(0)[0:1, :]),
                 waits=[(mm, sxmm[0])], inc=ac)
    a_sx1 = p.em("scalar", lambda e: e.copy(sxr[0:1, 512:1024], qbank(1)[0:1, :]),
                 waits=[(mm, sxmm[1])], inc=ac)

    def split_hilo_plan(src, scaled, hi_lo, lo1, scale, wait0):
        s1 = p.em("vector", lambda e: e.tensor_scalar_mul(
            scaled[0:1, :], src[0:1, :], scale), waits=wait0, inc=ve)
        s2 = p.em("vector", lambda e: e.tensor_copy(hi_lo[0:1, :], scaled[0:1, :]),
                  waits=[(ve, s1)], inc=ve)
        s3 = p.em("vector", lambda e: e.tensor_sub(
            scaled[0:1, :], scaled[0:1, :], hi_lo[0:1, :]), waits=[(ve, s2)], inc=ve)
        return p.em("vector", lambda e: e.tensor_copy(lo1[0:1, :], scaled[0:1, :]),
                    waits=[(ve, s3)], inc=ve)
    v_sx0 = split_hilo_plan(sxr, sxt, sxh, sxlo1, -128.0,
                            [(ac, a_sx0), (ac, a_sx1), (lg, misc6_ld)])
    v_sx = p.em("gpsimd", lambda e: e.dma_start(out=sxh[1:2, :], in_=sxlo1[0:1, :]),
                waits=[(ve, v_sx0)], inc=lg)
    sxh_ld = lg.n

    guval = [0] * HPB
    gu_read_mm = 0
    gtmp_free_ve = [0, 0]
    a_gate_prev = [a_sx0, a_sx1]
    up_prev_ac = [0, 0]
    wg_ld = {}
    wd_ld = {}
    wbuf_free = {"g": [0, 0], "u": [0, 0]}
    wd_slot_free = [0] * NWD
    last_down_evac_ve = 0
    v_sgu_prev = 0
    down_evac_count = 0

    p5_done_ls = ls.n
    p5_done_lg = lg.n

    def emit_wgu_load(hc):
        for tag, wsrc, dst in (("g", wg, wgs), ("u", wu, wus)):
            w = [(mm, wbuf_free[tag][hc % 2])]
            if hc < 2:
                w.extend(x3T_ready_w)
                w.append((ls, p5_done_ls))
                w.append((lg, p5_done_lg))
            wg_ld[(tag, hc)] = p.em("gpsimd", lambda e, wsrc=wsrc, hc=hc, buf=dst[hc % 2]:
                                    e.dma_start(out=buf[:, :], in_=bass.AP(
                                        wsrc, hc * 16 * 128 * 128,
                                        [[128, 128], [128 * 128, 16], [1, 128]])),
                                    waits=w, inc=lw)

    def emit_wd_load(hc):
        slot = hc % NWD
        w = [(mm, wd_slot_free[slot])]
        if hc < NWD:
            w.extend(x3T_ready_w)
            w.append((ls, p5_done_ls))
            w.append((lg, p5_done_lg))
        wd_ld[hc] = p.em("gpsimd", lambda e, hc=hc, buf=wds[slot]: e.dma_start(
            out=buf[:, :], in_=bass.AP(wd, hc * 128 * D, [[D, 128], [1, D]])),
            waits=w, inc=lw)

    emit_wgu_load(0)
    emit_wgu_load(1)
    emit_wd_load(0)

    x3f = x3T[:, :].bitcast(F32)
    tmpa7s = [x3f[:, 0:2048], x3f[:, 2048:4096]]
    tmpb7 = x3f[:, 4096:6144]
    ssdb = x3f[:, 6144:8192]
    p7state = {"sdb_ld": 0, "v_sgut": 0, "tmpa7_free": [(ac, 0), (ac, 0)],
               "tmpb7_free": 0, "fin_evac": [0] * 8}
    store_vals = []
    p7s = [dict() for _ in range(8)]

    def p7_a(tc):
        par = tc % 2
        tmpa7 = tmpa7s[par]
        p7s[tc]["fin"] = p.em("vector", lambda e, tc=tc, tmpa7=tmpa7, par=par:
                              e.scalar_tensor_tensor(
                                  tmpa7, down[:, tc * D:(tc + 1) * D], sguT[:, tc:tc + 1],
                                  ssdb, AluOpType.subtract, AluOpType.mult,
                                  accum_out=stx[:, par * 8:par * 8 + 1]),
            waits=[(ve, p7state["v_sgut"]), (ve, p7state["fin_evac"][tc]),
                   p7state["tmpa7_free"][par], (lg, p7state["sdb_ld"])], inc=ve)

    def p7_b(tc):
        p7s[tc]["ln4"] = ln_stats_chain(tmpa7s[tc % 2], (ve, p7s[tc]["fin"]),
                                        bank=tc % 2, presum=p7s[tc]["fin"])

    def p7_c(tc):
        par = tc % 2
        tmpa7 = tmpa7s[par]
        v_ln4 = ln_normalize(tmpb7, tmpa7, p7s[tc]["ln4"],
                             extra_waits=[(ve, p7state["tmpb7_free"])])
        x2ld = p.em("sync", lambda e, tc=tc, tmpa7=tmpa7: e.dma_start(
            out=tmpa7, in_=bass.AP(x2d, tc * 128 * D, [[D, 128], [1, D]])),
            waits=[(ls, x2_spill[tc]), (ve, v_ln4)], inc=ls)
        v_out = p.em("vector", lambda e, tmpa7=tmpa7: e.tensor_add(
            tmpa7, tmpa7, tmpb7), waits=[(ve, v_ln4), (ls, x2ld)], inc=ve)
        p7state["tmpb7_free"] = v_out
        sv = p.em("sync", lambda e, tc=tc, tmpa7=tmpa7: e.dma_start(
            out=bass.AP(out, tc * 128 * D, [[D, 128], [1, D]]), in_=tmpa7),
            waits=[(ve, v_out)], inc=ls)
        p7state["tmpa7_free"][par] = (ls, sv)
        p7s[tc]["store"] = sv
        store_vals.append(sv)

    for blk in range(NB):
        for k in range(HPB):
            hc = blk * HPB + k
            mmg = mmu = 0
            for qh in range(2):
                for dc in range(16):
                    w = []
                    if dc == 0:
                        w = [(lw, wg_ld[("g", hc)]), (ac, a_gate_prev[qh]), (lg, sxh_ld)]
                    p.em("tensor", lambda e, hc=hc, dc=dc, qh=qh: e.matmul(
                        qbank(qh), wgs[hc % 2][:, dc * 128:(dc + 1) * 128],
                        x3T[:, dc * TO + qh * 512: dc * TO + (qh + 1) * 512],
                        start=(dc == 0), stop=False), waits=w, inc=mm)
                mmg = p.em("tensor", lambda e, qh=qh: e.matmul(
                    qbank(qh), ones2[:, 0:128], sxh[:, qh * 512:(qh + 1) * 512],
                    start=False, stop=True), inc=mm)
                a_gate = p.em("scalar", lambda e, hc=hc, qh=qh: e.activation(
                    gtmp[:, qh * 512:(qh + 1) * 512], qbank(qh), AF.Silu,
                    scale=ssg[:, hc:hc + 1]),
                    waits=[(mm, mmg), (ve, gtmp_free_ve[qh])], inc=ac)
                a_gate_prev[qh] = a_gate
                for dc in range(16):
                    w = []
                    if dc == 0:
                        w = [(lw, wg_ld[("u", hc)]), (ve, guval[k]), (mm, gu_read_mm)]
                        if up_prev_ac[qh]:
                            w.append((ac, up_prev_ac[qh]))
                    p.em("tensor", lambda e, hc=hc, dc=dc, qh=qh: e.matmul(
                        qbank(2 + qh), wus[hc % 2][:, dc * 128:(dc + 1) * 128],
                        x3T[:, dc * TO + qh * 512: dc * TO + (qh + 1) * 512],
                        start=(dc == 0), stop=False), waits=w, inc=mm)
                mmu = p.em("tensor", lambda e, qh=qh: e.matmul(
                    qbank(2 + qh), ones2[:, 0:128], sxh[:, qh * 512:(qh + 1) * 512],
                    start=False, stop=True), inc=mm)
                up_prev_ac[qh] = 0
                gv = p.em("vector", lambda e, hc=hc, k=k, qh=qh: e.scalar_tensor_tensor(
                    gu[:, k * TO + qh * 512: k * TO + (qh + 1) * 512],
                    qbank(2 + qh), ssu[:, hc:hc + 1],
                    gtmp[:, qh * 512:(qh + 1) * 512],
                    AluOpType.mult, AluOpType.mult),
                    waits=[(mm, mmu), (ac, a_gate)], inc=ve)
                guval[k] = gv
                gtmp_free_ve[qh] = gv
            wbuf_free["g"][hc % 2] = mmg
            wbuf_free["u"][hc % 2] = mmu
            if hc + 2 < HC:
                emit_wgu_load(hc + 2)
            if k + 1 < HPB:
                emit_wd_load(hc + 1)
        smm = 0
        for k in range(HPB):
            for qh in range(2):
                w = [(ve, guval[k])]
                if k == 0 and qh == 0:
                    w.append((ve, v_sgu_prev))
                smm = p.em("tensor", lambda e, k=k, qh=qh: e.matmul(
                    [pav, psm][qh][:, :], sones128[:, :],
                    gu[:, k * TO + qh * 512: k * TO + (qh + 1) * 512],
                    start=(k == 0), stop=(k == HPB - 1)), waits=w, inc=mm)
        if blk == 0:
            v_sgu = p.em("vector", lambda e: e.tensor_copy(sgur[0:1, 0:512], pav[0:1, :]),
                         waits=[(mm, smm)], inc=ve)
            v_sgu = p.em("vector", lambda e: e.tensor_copy(sgur[0:1, 512:1024], psm[0:1, :]),
                         inc=ve)
        else:
            v_sgu = p.em("vector", lambda e: e.tensor_tensor(
                sgur[0:1, 0:512], sgur[0:1, 0:512], pav[0:1, :], AluOpType.add),
                waits=[(mm, smm)], inc=ve)
            v_sgu = p.em("vector", lambda e: e.tensor_tensor(
                sgur[0:1, 512:1024], sgur[0:1, 512:1024], psm[0:1, :], AluOpType.add),
                inc=ve)
        v_sgu_prev = v_sgu
        if blk == NB - 1:
            v_s1v = split_hilo_plan(sgur, sgur2, sguh, sxlo1, 128.0, [(ve, v_sgu)])
            v_s1d = p.em("gpsimd", lambda e: e.dma_start(out=sguh[1:2, :], in_=sxlo1[0:1, :]),
                         waits=[(ve, v_s1v)], inc=lg)
            for tcb in range(8):
                w = [(ve, v_s1v), (lg, v_s1d)] if tcb == 0 else []
                if tcb == 0:
                    w.append((ac, a_gate_prev[0]))
                    w.append((ac, a_gate_prev[1]))
                p.em("tensor", lambda e, tcb=tcb: e.matmul(
                    pA[:, tcb * 2:(tcb + 1) * 2], sguh[0:2, tcb * 128:(tcb + 1) * 128],
                    sid[0:2, 0:2], start=True, stop=True), waits=w, inc=mm)
            mml = mm.n
            a_sgt = p.em("scalar", lambda e: e.copy(stx[:, 0:16], pA[:, 0:16]),
                         waits=[(mm, mml)], inc=ac)
            p7state["v_sgut"] = p.em("vector", lambda e: e.tensor_tensor(
                bass.AP(sguT, 0, [[8, 128], [1, 8], [1, 1]]),
                bass.AP(stx, 0, [[32, 128], [2, 8], [1, 1]]),
                bass.AP(stx, 1, [[32, 128], [2, 8], [1, 1]]),
                AluOpType.add), waits=[(ac, a_sgt)], inc=ve)
            p7state["sdb_ld"] = p.em("gpsimd", lambda e: e.dma_start(
                out=ssdb, in_=sdb.ap()), waits=[(ve, p7state["v_sgut"])], inc=lg)
        dbank = [pt[0], pt[1], pav, psm]
        for tc in range(8):
            mmd = [0] * 4
            for k in range(HPB):
                hc = blk * HPB + k
                for dh in range(4):
                    w = []
                    if dh == 0:
                        w.append((lw, wd_ld[hc]))
                    if k == 0:
                        w.append((ve, last_down_evac_ve))
                        if dh >= 2:
                            w.append((ve, v_sgu))
                        if blk == 0 and tc == 0 and dh < 2:
                            w.append(evac_val5[dh])
                    mmd[dh] = p.em("tensor", lambda e, k=k, tc=tc, dh=dh, b=dbank[dh]:
                                   e.matmul(b[:, :],
                                            gu[:, k * TO + tc * 128: k * TO + (tc + 1) * 128],
                                            wds[(blk * HPB + k) % NWD][:, dh * 512:(dh + 1) * 512],
                                            start=(k == 0), stop=(k == HPB - 1)),
                                   waits=w, inc=mm)
            for dh in range(4):
                if blk == 0:
                    last_down_evac_ve = p.em("vector", lambda e, tc=tc, dh=dh, b=dbank[dh]:
                                             e.tensor_copy(
                                                 down[:, tc * D + dh * 512: tc * D + (dh + 1) * 512],
                                                 b[:, :]),
                                             waits=[(mm, mmd[dh])], inc=ve)
                else:
                    last_down_evac_ve = p.em("vector", lambda e, tc=tc, dh=dh, b=dbank[dh]:
                                             e.tensor_tensor(
                                                 down[:, tc * D + dh * 512: tc * D + (dh + 1) * 512],
                                                 down[:, tc * D + dh * 512: tc * D + (dh + 1) * 512],
                                                 b[:, :], AluOpType.add),
                                             waits=[(mm, mmd[dh])], inc=ve)
            gu_read_mm = mmd[3]
            for k in range(HPB):
                wd_slot_free[(blk * HPB + k) % NWD] = mmd[3]
            if blk == NB - 1:
                p7state["fin_evac"][tc] = last_down_evac_ve
                if tc >= 1:
                    p7_b(tc - 1)
                if tc >= 2:
                    p7_c(tc - 2)
                p7_a(tc)
        if (blk + 1) < NB:
            emit_wd_load((blk + 1) * HPB)

    ffn_done_ve = ve.n
    p7_b(7)
    p7_c(6)
    p7_c(7)
    tap("x3T", x3T[:, :], [128, 16 * TO], BF16, list(x3T_ready_w))
    tap("down", down[:, :], [128, 8 * D], F32, [(ve, ffn_done_ve)])
    tap("sguT", sguT[:, :], [128, 8], F32, [(ve, p7state["v_sgut"])])
    tap("ssdb", ssdb, [128, D], F32, [(lg, p7state["sdb_ld"])])
    tap("fin0", tmpa7s[0], [128, D], F32, [(ve, p7s[0]["fin"])])

    p.em("sync", lambda e: e.nop(), waits=[(ls, store_vals[-1]), (lg, lg.n)])
    p.flush(final=True)
    nc._dbg_tensors = dbg_tensors
    return nc


_NC_CACHE = None


def _get_nc():
    global _NC_CACHE
    if _NC_CACHE is None:
        _NC_CACHE = build_nc()
    return _NC_CACHE


def kernel(L0_attn_x, L0_attn_Wq, L0_attn_bq, L0_attn_Wk, L0_attn_bk,
           L0_attn_Wv, L0_attn_bv, L0_ffn_wq_gate, L0_ffn_s_gate,
           L0_ffn_wq_up, L0_ffn_s_up, L0_ffn_wq_down, L0_ffn_s_down):
    x = np.ascontiguousarray(np.asarray(L0_attn_x, dtype=np.float32))
    Wq = np.asarray(L0_attn_Wq, dtype=np.float32)
    Wk = np.asarray(L0_attn_Wk, dtype=np.float32)
    Wv = np.asarray(L0_attn_Wv, dtype=np.float32)
    bqv = np.asarray(L0_attn_bq, dtype=np.float32)
    bkv = np.asarray(L0_attn_bk, dtype=np.float32)
    bvv = np.asarray(L0_attn_bv, dtype=np.float32)
    wqg = np.asarray(L0_ffn_wq_gate, dtype=np.int32)
    wqu = np.asarray(L0_ffn_wq_up, dtype=np.int32)
    wqd = np.asarray(L0_ffn_wq_down, dtype=np.int32)
    sgv = np.asarray(L0_ffn_s_gate, dtype=np.float32)
    suv = np.asarray(L0_ffn_s_up, dtype=np.float32)
    sdv = np.asarray(L0_ffn_s_down, dtype=np.float32)

    def tile_qk(w):
        return np.ascontiguousarray(
            w.reshape(16, 128, 8, 128).transpose(2, 0, 1, 3))

    def tile_gu(wT):
        return np.ascontiguousarray(
            wT.reshape(16, 128, HC, 128).transpose(2, 0, 1, 3))

    wgT = np.ascontiguousarray(wqg.T)
    wuT = np.ascontiguousarray(wqu.T)
    wdT = np.ascontiguousarray(wqd.T)

    ident = np.eye(128, dtype=BF)
    sg_t = np.ascontiguousarray(sgv.reshape(HC, 128).T)
    su_t = np.ascontiguousarray(suv.reshape(HC, 128).T)

    perms = {hg: np.r_[hg * 1024:2048, 0:hg * 1024] for hg in range(2)}
    wg_t, wu_t, wd_t, sdb_t = {}, {}, {}, {}
    for hg in range(2):
        pm = perms[hg]
        wg_t[hg] = np.ascontiguousarray(tile_gu(wgT[pm]).astype(BF))
        wu_t[hg] = np.ascontiguousarray(tile_gu(wuT[pm]).astype(BF))
        wd_t[hg] = np.ascontiguousarray(wdT[:, pm].reshape(HC, 128, D).astype(BF))
        sdb_t[hg] = np.ascontiguousarray(np.broadcast_to(sdv[pm][None, :], (128, D)))

    in_maps = []
    for c in range(8):
        b, hg = c // 2, c % 2
        th = c % 2
        pm = perms[hg]
        sl = slice(hg * 1024, (hg + 1) * 1024)
        xb_c = np.ascontiguousarray(
            np.concatenate([x[b, th * TO:], x[b, :th * TO]], axis=0)[:, pm])
        in_maps.append(dict(
            xb=xb_c,
            xo=np.ascontiguousarray(xb_c[0:TO]),
            wq=tile_qk(Wq[pm][:, sl]),
            wk=tile_qk(Wk[pm][:, sl]),
            wv=np.ascontiguousarray(Wv[pm][:, sl].reshape(16, 128, 1024)),
            bq=np.ascontiguousarray(bqv[sl].reshape(8, 128).T),
            bk=np.ascontiguousarray(bkv[sl].reshape(8, 128).T),
            bvb=np.ascontiguousarray(np.broadcast_to(bvv[sl][None, :], (128, 1024))),
            wg=wg_t[hg], wu=wu_t[hg], wd=wd_t[hg],
            sg=sg_t, su=su_t, sdb=sdb_t[hg],
            ident=ident,
        ))

    nc = _get_nc()
    res = run_bass_kernel_spmd(nc, in_maps, core_ids=list(range(8)))

    outp = np.empty((4, T, D), dtype=np.float32)
    for c in range(8):
        b, hg = c // 2, c % 2
        th = c % 2
        outp[b, th * TO:(th + 1) * TO] = res.results[c]["out"][:, perms[hg]]
    return outp


# version 38
# speedup vs baseline: 1.0452x; 1.0452x over previous
import contextlib
import os

import numpy as np
import ml_dtypes
import concourse.bass as bass
import concourse.mybir as mybir
from concourse.alu_op_type import AluOpType
from concourse.bass_utils import run_bass_kernel_spmd

F32 = mybir.dt.float32
BF16 = mybir.dt.bfloat16
I32 = mybir.dt.int32
AF = mybir.ActivationFunctionType
AX = mybir.AxisListType

D = 2048
T = 2048
TO = 1024
H = 8
HID = 5632
HC = 44
NB = 4
HPB = 11
NWD = 11
SQ = 0.08838834764831845
EPS = 1e-5
BF = ml_dtypes.bfloat16


class Ctr:
    def __init__(self, h, step):
        self.h, self.step, self.n = h, step, 0

    def reserve(self):
        self.n += self.step
        return self.n


class Plan:

    def __init__(self, nc):
        self.nc = nc
        self.segments = [{k: [] for k in ("tensor", "scalar", "vector", "sync", "gpsimd")}]
        self.awaited = {}

    def em(self, eng, fn, waits=(), inc=None, inc_times=1):
        waits = tuple((c, v) for (c, v) in waits if v > 0)
        for ctr, val in waits:
            assert val <= ctr.n, f"forward wait: {val} > {ctr.n}"
            self.awaited.setdefault(id(ctr), set()).add(val)
        tgt = None
        if inc is not None:
            for _ in range(inc_times):
                tgt = inc.reserve()
        self.segments[-1][eng].append((fn, waits, inc, tgt))
        return tgt

    def flush(self, tag=None, final=False):
        skip = os.environ.get("KNOFLUSH", "").split(",")
        if final:
            self.emit_all()
        elif tag not in skip:
            self.segments.append(
                {k: [] for k in ("tensor", "scalar", "vector", "sync", "gpsimd")})

    def note_awaited(self, ctr, val):
        if val > 0:
            self.awaited.setdefault(id(ctr), set()).add(val)

    def emit_all(self):
        last_inc = {}

        waited_max = {}

        def runner_for(eng, lst):
            def runner(e, lst=lst, eng=eng):
                wm = waited_max.setdefault(eng, {})
                for fn, waits, inc, tgt in lst:
                    for ctr, val in waits:
                        if wm.get(id(ctr), -1) >= val:
                            continue
                        wm[id(ctr)] = val
                        e.wait_ge(ctr.h, val)
                    ins = fn(e)
                    if inc is None:
                        continue
                    if ins is None:
                        last_inc[id(inc)] = tgt
                        continue
                    delta = tgt - last_inc.get(id(inc), 0)
                    if (inc.step == 16 or delta >= 64
                            or tgt in self.awaited.get(id(inc), ())):
                        last_inc[id(inc)] = tgt
                        ins.then_inc(inc.h, delta)
            return runner
        for seg in self.segments:
            st = {k: v for k, v in seg.items() if v}
            if not st:
                continue
            with self.nc.Block() as blk:
                for eng, lst in st.items():
                    getattr(blk, eng)(runner_for(eng, lst))
        self.segments = [{k: [] for k in ("tensor", "scalar", "vector", "sync", "gpsimd")}]


DBG = bool(os.environ.get("KDBG"))


def build_nc():
    nc = bass.Bass("TRN2", target_bir_lowering=False)
    dbg_tensors = {}


    xb = nc.dram_tensor("xb", [T, D], F32, kind="ExternalInput")
    xo = nc.dram_tensor("xo", [TO, D], F32, kind="ExternalInput")
    wq = nc.dram_tensor("wq", [8, 16, 128, 128], F32, kind="ExternalInput")
    wk = nc.dram_tensor("wk", [8, 16, 128, 128], F32, kind="ExternalInput")
    wv = nc.dram_tensor("wv", [16, 128, 1024], F32, kind="ExternalInput")
    bq = nc.dram_tensor("bq", [128, 8], F32, kind="ExternalInput")
    bk = nc.dram_tensor("bk", [128, 8], F32, kind="ExternalInput")
    bvb = nc.dram_tensor("bvb", [128, 1024], F32, kind="ExternalInput")
    wg = nc.dram_tensor("wg", [HC, 16, 128, 128], BF16, kind="ExternalInput")
    wu = nc.dram_tensor("wu", [HC, 16, 128, 128], BF16, kind="ExternalInput")
    wd = nc.dram_tensor("wd", [HC, 128, D], BF16, kind="ExternalInput")
    sg = nc.dram_tensor("sg", [128, HC], F32, kind="ExternalInput")
    su = nc.dram_tensor("su", [128, HC], F32, kind="ExternalInput")
    sdb = nc.dram_tensor("sdb", [128, D], F32, kind="ExternalInput")
    ident = nc.dram_tensor("ident", [128, 128], BF16, kind="ExternalInput")
    out = nc.dram_tensor("out", [TO, D], F32, kind="ExternalOutput")

    cc_inb = nc.dram_tensor("cc_inb", [4, TO, 256], BF16, kind="Internal")
    cc_outb = nc.dram_tensor("cc_outb", [4, 2 * TO, 256], BF16, kind="Internal")
    x2d = nc.dram_tensor("x2d", [TO, D], F32, kind="Internal")
    atmd = nc.dram_tensor("atmd", [TO, 1024], BF16, kind="Internal")

    p = Plan(nc)

    def tap(name, sb_ap, shape, dtype, waits):
        if not DBG:
            return
        dt_ = nc.dram_tensor("dbg_" + name, shape, dtype, kind="ExternalOutput")
        dbg_tensors[name] = dt_
        p.em("sync", lambda e: e.dma_start(out=dt_.ap(), in_=sb_ap), waits=waits, inc=ls)

    ctx = contextlib.ExitStack()
    E = ctx.enter_context

    ls = Ctr(E(nc.semaphore("ls")), 16)
    lg = Ctr(E(nc.semaphore("lg")), 16)
    mm = Ctr(E(nc.semaphore("mm")), 1)
    ac = Ctr(E(nc.semaphore("ac")), 1)
    ve = Ctr(E(nc.semaphore("ve")), 1)
    cc = Ctr(E(nc.semaphore("cc")), 1)
    lw = Ctr(E(nc.semaphore("lw")), 16)
    pl = Ctr(E(nc.semaphore("pl")), 1)

    pt = [E(nc.psum_tensor(f"pt{i}", [128, 512], F32)) for i in range(2)]
    pA = E(nc.psum_tensor("pA", [128, 1024], F32))
    pB = E(nc.psum_tensor("pB", [128, 1024], F32))
    pav = E(nc.psum_tensor("pav", [128, 512], F32))
    psm = E(nc.psum_tensor("psm", [128, 512], F32))

    def qbank(qq):
        t_ = pA if qq < 2 else pB
        return t_[:, (qq % 2) * 512:(qq % 2 + 1) * 512]

    sid = E(nc.sbuf_tensor("sid", [128, 128], BF16))
    sonescol = E(nc.sbuf_tensor("sonescol", [128, 1], BF16))
    sones128 = E(nc.sbuf_tensor("sones128", [128, 128], BF16))
    sonesrow = E(nc.sbuf_tensor("sonesrow", [1, 128], BF16))
    sbq = E(nc.sbuf_tensor("sbq", [128, 8], F32))
    sbk = E(nc.sbuf_tensor("sbk", [128, 8], F32))
    sbqs = E(nc.sbuf_tensor("sbqs", [128, 8], F32))
    ssg = E(nc.sbuf_tensor("ssg", [128, HC], F32))
    ssu = E(nc.sbuf_tensor("ssu", [128, HC], F32))
    stx = E(nc.sbuf_tensor("stx", [128, 24], F32))
    epsc = E(nc.sbuf_tensor("epsc", [128, 1], F32))
    junkb = E(nc.sbuf_tensor("junkb", [128, D], BF16))
    negc = E(nc.sbuf_tensor("negc", [128, H], F32))
    maxs = E(nc.sbuf_tensor("maxs", [128, 1], F32))
    maxb = E(nc.sbuf_tensor("maxb", [128, 1], BF16))
    nc1 = E(nc.sbuf_tensor("nc1", [1, 1], F32))
    nc1b = E(nc.sbuf_tensor("nc1b", [1, 1], BF16))

    p.em("gpsimd", lambda e: e.dma_start(out=sid[:, :], in_=ident.ap()), inc=lg)
    p.em("gpsimd", lambda e: e.dma_start(out=sbq[:, :], in_=bq.ap()), inc=lg)
    p.em("gpsimd", lambda e: e.dma_start(out=sbk[:, :], in_=bk.ap()), inc=lg)
    p.em("gpsimd", lambda e: e.dma_start(out=ssg[:, :], in_=sg.ap()), inc=lg)
    p.em("gpsimd", lambda e: e.dma_start(out=ssu[:, :], in_=su.ap()), inc=lg)
    p.em("gpsimd", lambda e: e.memset(sonescol[:, :], 1.0), inc=lg)
    p.em("gpsimd", lambda e: e.memset(sones128[:, :], 1.0), inc=lg)
    p.em("gpsimd", lambda e: e.memset(sonesrow[:, :], 1.0), inc=lg)
    m_eps = p.em("gpsimd", lambda e: e.memset(epsc[:, :], EPS), inc=lg)
    const_ld = lg.n
    p.em("scalar", lambda e: e.mul(sbqs[:, :], sbq[:, :], SQ),
         waits=[(lg, const_ld)], inc=ac)

    def pipeline(n_tiles, stages, skew):
        slots = []
        for tt in range(n_tiles):
            for s in range(len(stages)):
                slots.append((tt * skew + s, -s, tt, s))
        slots.sort()
        for _, _, tt, s in slots:
            stages[s](tt)

    ln_state = {"prev_norm": [0, 0, 0]}

    def ln_stats_chain(xtile, ld_wait, bank=0, presum=None, mu_ap=None, rstd_ap=None):
        B0 = bank * 8

        def c(i):
            return stx[:, B0 + i:B0 + i + 1]
        muA = c(2) if mu_ap is None else mu_ap
        rsA = c(6) if rstd_ap is None else rstd_ap
        if presum is None:
            v_sum = p.em("vector", lambda e, x=xtile: e.reduce_sum(c(0), x[:, :], AX.X),
                         waits=[ld_wait], inc=ve)
            sum_wait = (ve, v_sum)
        else:
            sum_wait = (ve, presum)
        a_sq = p.em("scalar", lambda e, x=xtile: e.activation(
            junkb[:, :], x[:, :], AF.Square, accum_out=c(1)),
            waits=[ld_wait, (ve, ln_state["prev_norm"][bank])], inc=ac)
        sMu = p.em("scalar", lambda e: e.activation(muA, c(0), AF.Identity, scale=1.0 / D),
                   waits=[sum_wait, (ac, a_sq)], inc=ac)
        sC = p.em("scalar", lambda e: e.activation(c(4), c(0), AF.Square, scale=1.0 / D),
                  waits=[sum_wait, (ac, sMu)], inc=ac)
        sEx = p.em("scalar", lambda e: e.activation(c(3), c(1), AF.Identity,
                                                    scale=1.0 / D, bias=epsc[:, :]),
                   waits=[(ac, sC), (lg, m_eps)], inc=ac)
        sD = p.em("scalar", lambda e: e.activation(c(5), c(4), AF.Identity,
                                                   scale=-1.0, bias=c(3)),
                  waits=[(ac, sEx)], inc=ac)
        vE = p.em("vector", lambda e: e.reciprocal(c(5), c(5)),
                  waits=[(ac, sD)], inc=ve)
        a_rstd = p.em("scalar", lambda e: e.activation(rsA, c(5), AF.Sqrt),
                      waits=[(ve, vE)], inc=ac)
        return a_rstd, muA, rsA, bank

    def ln_normalize(out_ap, in_ap, stats, extra_waits=()):
        a_rstd, mu_ap, rstd_ap, bank = stats
        v_norm = p.em("vector", lambda e: e.tensor_scalar(
            out_ap, in_ap, mu_ap, rstd_ap, AluOpType.subtract, AluOpType.mult),
            waits=[(ac, a_rstd)] + list(extra_waits), inc=ve)
        ln_state["prev_norm"][bank] = v_norm
        return v_norm

    statbuf = E(nc.sbuf_tensor("statbuf", [128, 32], F32))

    scopeB2 = contextlib.ExitStack()
    EB = scopeB2.enter_context
    qT = EB(nc.sbuf_tensor("qT", [128, 8 * T], BF16))
    kT = EB(nc.sbuf_tensor("kT", [128, 8 * T], BF16))
    vsb = EB(nc.sbuf_tensor("vsb", [128, 16 * 1024], BF16))

    scopeA2 = contextlib.ExitStack()
    x1T = scopeA2.enter_context(nc.sbuf_tensor("x1T", [128, 16 * T], BF16))

    scopeA1 = contextlib.ExitStack()
    xin = [scopeA1.enter_context(nc.sbuf_tensor(f"xin{i}", [128, D], F32)) for i in range(3)]
    x1bs = [scopeA1.enter_context(nc.sbuf_tensor(f"x1b{i}", [128, D], BF16)) for i in range(3)]

    evac_val = [(ve, 0), (ac, 0)]
    xin_free = [(0, 0)] * 3
    x1b_free_mm = [0, 0, 0]
    p1s = [dict() for _ in range(16)]

    def p1_load(tt):
        buf = xin[tt % 3]
        fa, fv = xin_free[tt % 3]
        eng = "sync" if tt % 2 == 0 else "gpsimd"
        ctr = ls if tt % 2 == 0 else lg
        p1s[tt]["ld"] = (ctr, p.em(eng, lambda e, tt=tt, buf=buf: e.dma_start(
            out=buf[:, :], in_=bass.AP(xb, tt * 128 * D, [[D, 128], [1, D]])),
            waits=[(ac, fa), (ve, fv)], inc=ctr))

    def p1_stats(tt):
        buf = xin[tt % 3]
        p1s[tt]["ln"] = ln_stats_chain(
            buf, p1s[tt]["ld"], bank=tt % 3,
            mu_ap=statbuf[:, 2 * tt:2 * tt + 1], rstd_ap=statbuf[:, 2 * tt + 1:2 * tt + 2])

    def p1_norm(tt):
        buf = xin[tt % 3]
        stats = p1s[tt]["ln"]
        x1b = x1bs[tt % 3]
        v_norm = ln_normalize(x1b[:, :], buf[:, :], stats,
                              extra_waits=[(mm, x1b_free_mm[tt % 3])])
        xin_free[tt % 3] = (stats[0], v_norm)
        for g in range(4):
            bank = pt[g % 2]
            mmlast = None
            for j in range(4):
                dc = g * 4 + j
                w = []
                if j == 0:
                    w = [(ve, v_norm), evac_val[g % 2]]
                mmlast = p.em("tensor", lambda e, bank=bank, x1b=x1b, j=j, dc=dc: e.matmul(
                    bank[:, j * 128:(j + 1) * 128], x1b[:, dc * 128:(dc + 1) * 128],
                    sid[:, :], start=True, stop=True), waits=w, inc=mm)
            dst = bass.AP(x1T, (g * 4) * T + tt * 128, [[16 * T, 128], [T, 4], [1, 128]])
            eng = "vector" if g % 2 == 0 else "scalar"
            ctr = ve if g % 2 == 0 else ac
            if g % 2 == 0:
                evac_val[0] = (ve, p.em("vector", lambda e, bank=bank, dst=dst:
                                        e.tensor_copy(dst, bank[:, :]),
                                        waits=[(mm, mmlast)], inc=ve))
            else:
                evac_val[1] = (ac, p.em("scalar", lambda e, bank=bank, dst=dst:
                                        e.copy(dst, bank[:, :]),
                                        waits=[(mm, mmlast)], inc=ac))
        x1b_free_mm[tt % 3] = mm.n

    pipeline(16, [p1_load, p1_stats, p1_norm], skew=1)

    x1T_ready_ve = ve.n
    x1T_ready_ac = ac.n
    p1_done_mm = mm.n
    tap("x1T", x1T[:, :], [128, 16 * T], BF16, [(ve, ve.n)])
    p1_done_ac = ac.n
    p1_done_ve = ve.n
    p.flush("p1")
    scopeA1.close()

    scopeB1 = contextlib.ExitStack()
    wqs = [scopeB1.enter_context(nc.sbuf_tensor(f"wqs{i}", [128, 2048], BF16))
           for i in range(2)]

    qk_jobs = [(wq, qT, sbqs, SQ), (wk, kT, sbk, 1.0)]
    qk_evac = {}
    pq_free = [0] * 4
    slab_free = [0, 0]
    slab_ld = {}
    for i in range(2):
        j, oc = divmod(i, 8)
        slab_ld[i] = p.em("gpsimd", lambda e, w=qk_jobs[j][0], oc=oc, buf=wqs[i % 2]:
                          e.dma_start(out=buf[:, :], in_=bass.AP(
                              w, oc * 16 * 128 * 128,
                              [[128, 128], [128 * 128, 16], [1, 128]])),
                          waits=[(ac, p1_done_ac), (ve, p1_done_ve),
                                 (mm, p1_done_mm)], inc=lg)
    last_k_mm = 0
    for j, (wsrc, dstT, biasap, scale) in enumerate(qk_jobs):
        for oc in range(8):
            i = j * 8 + oc
            buf = wqs[i % 2]
            mml = [0] * 4
            for dc in range(16):
                for qq in range(4):
                    w = []
                    if dc == 0:
                        w = [(lg, slab_ld[i]), (ac, pq_free[qq])]
                        if i == 0 and qq == 0:
                            w.append((ve, x1T_ready_ve))
                            w.append((ac, x1T_ready_ac))
                    mml[qq] = p.em("tensor", lambda e, buf=buf, dc=dc, qq=qq, b=qbank(qq):
                                   e.matmul(b, buf[:, dc * 128:(dc + 1) * 128],
                                            x1T[:, dc * T + qq * 512: dc * T + (qq + 1) * 512],
                                            start=(dc == 0), stop=(dc == 15)),
                                   waits=w, inc=mm)
            last_k_mm = mml[3]
            for qq in range(4):
                a = p.em("scalar", lambda e, dstT=dstT, oc=oc, qq=qq, b=qbank(qq),
                         biasap=biasap, scale=scale: e.activation(
                             dstT[:, oc * T + qq * 512: oc * T + (qq + 1) * 512],
                             b, AF.Identity, bias=biasap[:, oc:oc + 1], scale=scale),
                         waits=[(mm, mml[qq])], inc=ac)
                pq_free[qq] = a
                qk_evac[(j, oc)] = a
            slab_free[i % 2] = ac.n
            if i + 2 < 16:
                j2, oc2 = divmod(i + 2, 8)
                slab_ld[i + 2] = p.em("gpsimd", lambda e, w=qk_jobs[j2][0], oc2=oc2,
                                      buf=wqs[i % 2]: e.dma_start(
                                          out=buf[:, :], in_=bass.AP(
                                              w, oc2 * 16 * 128 * 128,
                                              [[128, 128], [128 * 128, 16], [1, 128]])),
                                      waits=[(ac, slab_free[i % 2])], inc=lg)

    negc_ready = [0] * H
    pp = {"bank_free": [evac_val[0], evac_val[1]], "maxs_ac": 0, "maxb_mm": 0,
          "nc1_ac": 0, "nc1b_mm": 0}
    pp_state = [dict() for _ in range(H)]

    def pp0(h):
        bank = pt[h % 2]
        w = [(ac, qk_evac[(0, h)]), (ac, qk_evac[(1, h)]), pp["bank_free"][h % 2]]
        pp_state[h]["m1"] = p.em("tensor", lambda e, h=h, bank=bank: e.matmul(
            bank[:, :], kT[:, h * T: h * T + 128], qT[:, h * T: h * T + 512],
            start=True, stop=True), waits=w, inc=mm)

    def pp1(h):
        bank = pt[h % 2]
        v1 = p.em("vector", lambda e, bank=bank: e.reduce_max(maxs[:, 0:1], bank[:, :], AX.X),
                  waits=[(mm, pp_state[h]["m1"]), (ac, pp["maxs_ac"])], inc=ve)
        a1 = p.em("scalar", lambda e: e.copy(maxb[:, :], maxs[:, 0:1]),
                  waits=[(ve, v1), (mm, pp["maxb_mm"])], inc=ac)
        pp["maxs_ac"] = a1
        pp_state[h]["a1"] = a1

    def pp2(h):
        bank = pt[h % 2]
        m2 = p.em("tensor", lambda e, bank=bank: e.matmul(
            bank[0:1, 0:128], maxb[:, :], sid[:, :], start=True, stop=True),
            waits=[(ac, pp_state[h]["a1"])], inc=mm)
        pp["maxb_mm"] = m2
        v2 = p.em("vector", lambda e, bank=bank: e.reduce_max(
            nc1[0:1, 0:1], bank[0:1, 0:128], AX.X, negate=True),
            waits=[(mm, m2), (ac, pp["nc1_ac"])], inc=ve)
        a2 = p.em("scalar", lambda e: e.copy(nc1b[:, :], nc1[0:1, 0:1]),
                  waits=[(ve, v2), (mm, pp["nc1b_mm"])], inc=ac)
        pp_state[h]["a2"] = a2
        pp["nc1_ac"] = a2

    def pp3(h):
        bank = pt[h % 2]
        m3 = p.em("tensor", lambda e, bank=bank: e.matmul(
            bank[:, 300:301], sonesrow[:, :], nc1b[:, :], start=True, stop=True),
            waits=[(ac, pp_state[h]["a2"])], inc=mm)
        pp["nc1b_mm"] = m3
        a3 = p.em("scalar", lambda e, h=h, bank=bank: e.copy(negc[:, h:h + 1], bank[:, 300:301]),
                  waits=[(mm, m3)], inc=ac)
        pp["bank_free"][h % 2] = (ac, a3)
        negc_ready[h] = a3

    pp_stages = [pp0, pp1, pp2, pp3]
    pp_slots = {}
    for h in range(H):
        for k in range(4):
            pp_slots.setdefault(2 * h + k, []).append((4 - k, h, k))
    for v in pp_slots.values():
        v.sort()

    def pp_emit_slot(t):
        for _, h, k in pp_slots.get(t, []):
            pp_stages[k](h)

    scopeB1b = contextlib.ExitStack()
    wvf = scopeB1b.enter_context(nc.sbuf_tensor("wvf", [128, 16 * 1024], BF16))
    sbvb = scopeB1b.enter_context(nc.sbuf_tensor("sbvb", [128, 1024], BF16))
    p.em("gpsimd", lambda e: e.dma_start(out=sbvb[:, :], in_=bvb.ap()), inc=lg)
    for dc in range(16):
        p.em("gpsimd", lambda e, dc=dc: e.dma_start(
            out=wvf[:, dc * 1024:(dc + 1) * 1024],
            in_=bass.AP(wv, dc * 128 * 1024, [[1024, 128], [1, 1024]])), inc=lg)
    wv_ld = lg.n
    v_evac_prev = 0
    for tc in range(16):
        mml = [0, 0]
        for dc in range(16):
            for half in range(2):
                w = []
                if dc == 0:
                    w = [(lg, wv_ld), (ve, v_evac_prev), (ac, pq_free[2 + half])]
                mml[half] = p.em("tensor", lambda e, dc=dc, tc=tc, half=half, b=[pav, psm][half]:
                                 e.matmul(b[:, :],
                                          x1T[:, dc * T + tc * 128: dc * T + (tc + 1) * 128],
                                          wvf[:, dc * 1024 + half * 512: dc * 1024 + (half + 1) * 512],
                                          start=(dc == 0), stop=(dc == 15)),
                                 waits=w, inc=mm)
        for half in range(2):
            v_evac_prev = p.em("vector", lambda e, tc=tc, half=half, b=[pav, psm][half]:
                               e.tensor_tensor(
                                   vsb[:, tc * 1024 + half * 512: tc * 1024 + (half + 1) * 512],
                                   b[:, :], sbvb[:, half * 512:(half + 1) * 512],
                                   AluOpType.add),
                               waits=[(mm, mml[half])], inc=ve)
        pp_emit_slot(tc)

    tap("qT", qT[:, :], [128, 8 * T], BF16, [(ac, ac.n)])
    tap("kT", kT[:, :], [128, 8 * T], BF16, [(ac, ac.n)])
    tap("vsb", vsb[:, :], [128, 16 * 1024], BF16, [(ve, ve.n)])
    for t in range(16, 20):
        pp_emit_slot(t)
    p.flush("v")
    scopeB1b.close()
    scopeB1.close()
    scopeA2.close()

    scopeC = contextlib.ExitStack()
    EC = scopeC.enter_context
    expT = EC(nc.sbuf_tensor("expT", [128, 16 * 512], BF16))
    avhs = [EC(nc.sbuf_tensor(f"avh{i}", [128, T], BF16)) for i in range(2)]
    atm_peer = EC(nc.sbuf_tensor("atm_peer", [128, 8 * 1024], BF16))
    atm_own = EC(nc.sbuf_tensor("atm_own", [128, 8 * 1024], BF16))
    sumsrows = [EC(nc.sbuf_tensor(f"sumsrow{i}", [1, T], F32)) for i in range(2)]
    sumhi = EC(nc.sbuf_tensor("sumhi", [1, T], BF16))
    sumlo32 = EC(nc.sbuf_tensor("sumlo32", [1, T], F32))
    sumlo = EC(nc.sbuf_tensor("sumlo", [1, T], BF16))
    recipTs = [EC(nc.sbuf_tensor(f"recipT{i}", [128, 16], F32)) for i in range(2)]

    PAIRS = [pA, pB]
    exp_pair_val = [pq_free[1], pq_free[3]]
    onesval = [0] * 16
    avcopy_prev = v_evac_prev
    avcopy_prev_ac = 0
    sumscopy_prev_ac = 0
    recip_guard = [pp["bank_free"][0], pp["bank_free"][1]]
    pt_norm = [0, 0]
    epi_pending = None

    epi_state = {"sums_tr_mm": 0}

    def emit_epilogue(hp, sums_ac, av_ac):
        par = hp % 2
        sumsrow = sumsrows[par]
        recipT = recipTs[par]
        avh = avhs[par]
        v1 = p.em("vector", lambda e: e.tensor_copy(sumhi[:, :], sumsrow[:, :]),
                  waits=[(ac, sums_ac), (mm, epi_state["sums_tr_mm"])], inc=ve)
        v2 = p.em("vector", lambda e: e.tensor_sub(sumlo32[:, :], sumsrow[:, :], sumhi[:, :]),
                  waits=[(ve, v1)], inc=ve)
        v3 = p.em("vector", lambda e: e.tensor_copy(sumlo[:, :], sumlo32[:, :]),
                  waits=[(ve, v2)], inc=ve)
        mml = 0
        for qt in range(16):
            w = [(ve, v3), recip_guard[0]] if qt == 0 else []
            p.em("tensor", lambda e, qt=qt: e.matmul(
                pt[0][:, qt:qt + 1], sumhi[0:1, qt * 128:(qt + 1) * 128],
                sonesrow[0:1, 0:1], start=True, stop=True), waits=w, inc=mm)
            mml = p.em("tensor", lambda e, qt=qt: e.matmul(
                pt[0][:, qt:qt + 1], sumlo[0:1, qt * 128:(qt + 1) * 128],
                sonesrow[0:1, 0:1], start=False, stop=True), inc=mm)
        epi_state["sums_tr_mm"] = mml
        rv = p.em("vector", lambda e, recipT=recipT: e.reciprocal(
            recipT[:, 0:16], pt[0][:, 0:16]), waits=[(mm, mml)], inc=ve)
        for qt in range(16):
            bank = pt[qt % 2]
            w = [(ve, pt_norm[qt % 2]), (ac, av_ac)]
            if qt == 0:
                w.append((ve, rv))
            if qt == 1:
                w.append(recip_guard[1])
            m1 = p.em("tensor", lambda e, qt=qt, bank=bank, avh=avh: e.matmul(
                bank[:, 0:128], avh[:, qt * 128:(qt + 1) * 128], sid[:, :],
                start=True, stop=True), waits=w, inc=mm)
            dst = (atm_own[:, qt * 1024 + hp * 128: qt * 1024 + (hp + 1) * 128]
                   if qt < 8 else
                   atm_peer[:, (qt - 8) * 1024 + hp * 128: (qt - 8) * 1024 + (hp + 1) * 128])
            pt_norm[qt % 2] = p.em("vector",
                                   lambda e, qt=qt, dst=dst, bank=bank, recipT=recipT:
                                   e.tensor_scalar(
                                       dst,
                                       bank[:, 0:128], recipT[:, qt:qt + 1], None,
                                       AluOpType.mult),
                                   waits=[(mm, m1)], inc=ve)
        recip_guard[1] = (ve, pt_norm[1])
        recip_guard[0] = (ve, pt_norm[0])

    attn_q_ve = [0, 0, 0, 0]
    for h in range(H):
        avh = avhs[h % 2]
        sumsrow = sumsrows[h % 2]
        for qb in range(4):
            scv = [0] * 16
            for mp in range(8):
                mj0 = 2 * mp
                P = mp % 2
                bankT = PAIRS[P]
                for k in range(2):
                    mj = mj0 + k
                    w = []
                    if k == 0:
                        w.append((ac, exp_pair_val[P]))
                        if mj0 == 0:
                            w.append((ac, negc_ready[h]))
                    scv[mj] = p.em("tensor", lambda e, h=h, qb=qb, mj=mj, k=k, bankT=bankT:
                                   e.matmul(
                                       bankT[:, k * 512:(k + 1) * 512],
                                       kT[:, h * T + mj * 128: h * T + (mj + 1) * 128],
                                       qT[:, h * T + qb * 512: h * T + (qb + 1) * 512],
                                       start=True, stop=True), waits=w, inc=mm)
                we = [(mm, scv[mj0 + 1]), (mm, onesval[mj0]), (mm, onesval[mj0 + 1])]
                if h == 0 and qb == 0 and mp == 0:
                    we.append((ve, v_evac_prev))
                exp_pair_val[P] = p.em("scalar", lambda e, mj0=mj0, bankT=bankT, h=h:
                                       e.activation(
                                           expT[:, mj0 * 512:(mj0 + 2) * 512],
                                           bankT[:, :], AF.Exp, bias=negc[:, h:h + 1]),
                                       waits=we, inc=ac)
                if mp >= 1:
                    for mj in (mj0 - 2, mj0 - 1):
                        wa = [(ac, exp_pair_val[(mp - 1) % 2])]
                        if mj == 0:
                            wa += [(ve, avcopy_prev), (ac, avcopy_prev_ac)]
                        p.em("tensor", lambda e, h=h, mj=mj: e.matmul(
                            pav[:, :], vsb[:, mj * 1024 + h * 128: mj * 1024 + (h + 1) * 128],
                            expT[:, mj * 512:(mj + 1) * 512],
                            start=(mj == 0), stop=(mj == 15)), waits=wa, inc=mm)
                        wo = []
                        if mj == 0:
                            wo = [(ac, sumscopy_prev_ac)]
                        onesval[mj] = p.em("tensor", lambda e, mj=mj: e.matmul(
                            psm[:, :], sones128[:, :], expT[:, mj * 512:(mj + 1) * 512],
                            start=(mj == 0), stop=(mj == 15)), waits=wo, inc=mm)
            for mj in (14, 15):
                wa = [(ac, exp_pair_val[1])]
                p.em("tensor", lambda e, h=h, mj=mj: e.matmul(
                    pav[:, :], vsb[:, mj * 1024 + h * 128: mj * 1024 + (h + 1) * 128],
                    expT[:, mj * 512:(mj + 1) * 512],
                    start=(mj == 0), stop=(mj == 15)), waits=wa, inc=mm)
                onesval[mj] = p.em("tensor", lambda e, mj=mj: e.matmul(
                    psm[:, :], sones128[:, :], expT[:, mj * 512:(mj + 1) * 512],
                    start=(mj == 0), stop=(mj == 15)), inc=mm)
            avcopy_prev_ac = p.em("scalar", lambda e, qb=qb, avh=avh: e.copy(
                avh[:, qb * 512:(qb + 1) * 512], pav[:, :]),
                waits=[(mm, onesval[15])], inc=ac)
            sumscopy_prev_ac = p.em("scalar", lambda e, qb=qb, sumsrow=sumsrow: e.copy(
                sumsrow[0:1, qb * 512:(qb + 1) * 512], psm[0:1, :]),
                waits=[(mm, onesval[15])], inc=ac)
            avcopy_prev = 0
            if qb == 0 and epi_pending is not None:
                hp = epi_pending[0]
                emit_epilogue(*epi_pending)
                epi_pending = None
                if hp % 2 == 1:
                    attn_q_ve[hp // 2] = ve.n
        epi_pending = (h, sumscopy_prev_ac, avcopy_prev_ac)
    emit_epilogue(*epi_pending)
    attn_q_ve[3] = ve.n

    attn_done_ve = ve.n
    tap("negc", negc[:, :], [128, H], F32, [(ac, ac.n)])
    tap("atm_own", atm_own[:, :], [128, 8 * 1024], BF16, [(ve, attn_done_ve)])
    tap("atm_peer", atm_peer[:, :], [128, 8 * 1024], BF16, [(ve, attn_done_ve)])

    cc_done = [0, 0, 0, 0]
    st_cc = 0
    for s in range(4):
        st_cc = p.em("gpsimd", lambda e, s=s: e.dma_start(
            out=bass.AP(cc_inb, s * TO * 256, [[256, 128], [128 * 256, 8], [1, 256]]),
            in_=bass.AP(atm_peer, s * 256, [[8 * 1024, 128], [1024, 8], [1, 256]])),
            waits=[(ve, attn_q_ve[s])], inc=lg)
        cc_done[s] = p.em("gpsimd", lambda e, s=s: e.collective_compute(
            "AllGather", AluOpType.bypass,
            replica_groups=[[0, 1], [2, 3], [4, 5], [6, 7]],
            ins=[bass.AP(cc_inb, s * TO * 256, [[256, TO], [1, 256]]).opt()],
            outs=[bass.AP(cc_outb, s * 2 * TO * 256, [[256, 2 * TO], [1, 256]]).opt()]),
            waits=[(lg, st_cc)], inc=cc)
        p.note_awaited(cc, cc_done[s])
    cc_send_done = st_cc
    spill_own = p.em("gpsimd", lambda e: e.dma_start(
        out=bass.AP(atmd, 0, [[1024, 128], [128 * 1024, 8], [1, 1024]]),
        in_=bass.AP(atm_own, 0, [[8 * 1024, 128], [1024, 8], [1, 1024]])),
        waits=[(ve, attn_done_ve)], inc=lg)

    p.flush("p4")
    scopeC.close()
    scopeB2.close()

    scopeE2 = contextlib.ExitStack()
    E2 = scopeE2.enter_context
    down = E2(nc.sbuf_tensor("down", [128, 8 * D], F32))
    sguT = E2(nc.sbuf_tensor("sguT", [128, 8], F32))

    sgur = sgur2 = sguh = None
    scopeX3T = contextlib.ExitStack()
    x3T = scopeX3T.enter_context(nc.sbuf_tensor("x3T", [128, 16 * TO], BF16))

    scopeD = contextlib.ExitStack()
    ED = scopeD.enter_context
    afp = [ED(nc.sbuf_tensor(f"afp{i}", [128, 1024], BF16)) for i in range(3)]
    afo = [ED(nc.sbuf_tensor(f"afo{i}", [128, 1024], BF16)) for i in range(3)]
    xol = [ED(nc.sbuf_tensor(f"xol{i}", [128, D], F32)) for i in range(3)]
    tmpas = [ED(nc.sbuf_tensor(f"tmpa{i}", [128, D], F32)) for i in range(3)]
    tmpbs = [ED(nc.sbuf_tensor(f"tmpb{i}", [128, D], F32)) for i in range(2)]
    x3bs = [ED(nc.sbuf_tensor(f"x3b{i}", [128, D], BF16)) for i in range(2)]

    rb_vals = [0] * 8
    afp_free = [0, 0, 0]
    own_ld = [0] * 8
    afo_free = [0, 0, 0]
    mrg_prev = [0, 0, 0]

    def emit_rb(tt):
        def fn(e, tt=tt):
            pid = e.partition_id()
            with e.register(f"off{tt}") as off:
                e.reg_alu(off, pid, 1, AluOpType.bitwise_and)
                e.reg_alu(off, off, 1, AluOpType.bitwise_xor)
                e.reg_mul(off, off, TO * 256)
                e.reg_add(off, off, tt * 128 * 256)
                e.dma_start(
                    out=bass.AP(afp[tt % 3], 0, [[1024, 128], [256, 4], [1, 256]]),
                    in_=bass.AP(cc_outb, off,
                                [[256, 128], [2 * TO * 256, 4], [1, 256]])
                ).then_inc(lg.h, 16)
            return None
        rb_vals[tt] = p.em("gpsimd", fn,
                           waits=[(cc, cc_done[3]), (pl, afp_free[tt % 3])],
                           inc=lg)

    xol_free = [(0, 0)] * 3
    evac_val5 = [(ve, 0), (ac, 0)]
    x3b_free_mm = [0, 0]
    tmpa_free = [[], [], []]
    tmpb_free_ve = [0, 0]
    x2_spill = [0] * 8
    x3T_half_w = [None, None]
    p5s = [dict() for _ in range(8)]

    def p5_load(tt):
        par = tt % 3
        buf = xol[par]
        fa, fv = xol_free[par]
        w = [(ac, fa), (ve, fv)]
        if tt < 2:
            w.append((ve, attn_done_ve))
            w.append((lg, cc_send_done))
        p5s[tt]["ld"] = (ls, p.em("sync", lambda e, tt=tt, buf=buf: e.dma_start(
            out=buf[:, :], in_=bass.AP(xo, tt * 128 * D, [[D, 128], [1, D]])),
            waits=w, inc=ls))
        emit_rb(tt)
        own_ld[tt] = p.em("sync", lambda e, tt=tt: e.dma_start(
            out=afo[tt % 3][:, :],
            in_=bass.AP(atmd, tt * 128 * 1024, [[1024, 128], [1, 1024]])),
            waits=[(lg, spill_own), (pl, afo_free[tt % 3])], inc=ls)

    def p5_attio(tt):
        par = tt % 3
        buf, tmpa = xol[par], tmpas[par]
        w = [p5s[tt]["ld"]] + tmpa_free[par]
        if tt < 3:
            w.append((ve, attn_done_ve))
            w.append((lg, cc_send_done))
        v_x1 = p.em("vector", lambda e, buf=buf, tmpa=tmpa, tt=tt: e.tensor_scalar(
            tmpa[:, :], buf[:, :], statbuf[:, 2 * tt:2 * tt + 1],
            statbuf[:, 2 * tt + 1:2 * tt + 2],
            AluOpType.subtract, AluOpType.mult), waits=w, inc=ve)
        a_sum_own = p.em("scalar", lambda e, tt=tt: e.activation(
            junkb[:, 0:1024], afo[tt % 3][:, :], AF.Identity,
            accum_out=stx[:, (tt % 3) * 8:(tt % 3) * 8 + 1]),
            waits=[(ls, own_ld[tt]), (ve, mrg_prev[par])], inc=ac)
        a_sum_peer = p.em("scalar", lambda e, tt=tt: e.activation(
            junkb[:, 1024:2048], afp[tt % 3][:, :], AF.Identity,
            accum_out=stx[:, (tt % 3) * 8 + 7:(tt % 3) * 8 + 8]),
            waits=[(lg, rb_vals[tt]), (ac, a_sum_own)], inc=ac)
        g_own = p.em("gpsimd", lambda e, tt=tt, tmpa=tmpa: e.tensor_tensor(
            tmpa[:, 0:1024], tmpa[:, 0:1024], afo[tt % 3][:, :], AluOpType.add),
            waits=[(ve, v_x1), (ls, own_ld[tt])], inc=pl)
        afo_free[tt % 3] = g_own
        g_peer = p.em("gpsimd", lambda e, tt=tt, tmpa=tmpa: e.tensor_tensor(
            tmpa[:, 1024:2048], tmpa[:, 1024:2048], afp[tt % 3][:, :], AluOpType.add),
            waits=[(lg, rb_vals[tt]), (pl, g_own)], inc=pl)
        afp_free[tt % 3] = g_peer
        p5s[tt]["pool"] = g_peer
        v_mrg = p.em("vector", lambda e, tt=tt: e.tensor_tensor(
            stx[:, (tt % 3) * 8:(tt % 3) * 8 + 1],
            stx[:, (tt % 3) * 8:(tt % 3) * 8 + 1],
            stx[:, (tt % 3) * 8 + 7:(tt % 3) * 8 + 8], AluOpType.add),
            waits=[(ac, a_sum_peer)], inc=ve)
        mrg_prev[par] = v_mrg
        p5s[tt]["attio"] = v_mrg

    def p5_ln2(tt):
        p5s[tt]["ln2"] = ln_stats_chain(tmpas[tt % 3], (pl, p5s[tt]["pool"]),
                                        bank=tt % 3, presum=p5s[tt]["attio"])

    def p5_x2(tt):
        par = tt % 3
        buf, tmpa, tmpb = xol[par], tmpas[par], tmpbs[tt % 2]
        v_ln2 = ln_normalize(tmpb[:, :], tmpa[:, :], p5s[tt]["ln2"],
                             extra_waits=[(ve, tmpb_free_ve[tt % 2])])
        v_x2 = p.em("vector", lambda e, buf=buf, tmpa=tmpa, tmpb=tmpb: e.scalar_tensor_tensor(
            tmpa[:, :], buf[:, :], 0.0, tmpb[:, :], AluOpType.add, AluOpType.add,
            accum_out=stx[:, par * 8:par * 8 + 1]), waits=[(ve, v_ln2)], inc=ve)
        tmpb_free_ve[tt % 2] = v_x2
        p5s[tt]["x2"] = v_x2
        x2_spill[tt] = p.em("sync", lambda e, tt=tt, tmpa=tmpa: e.dma_start(
            out=bass.AP(x2d, tt * 128 * D, [[D, 128], [1, D]]), in_=tmpa[:, :]),
            waits=[(ve, v_x2)], inc=ls)

    def p5_ln3(tt):
        p5s[tt]["ln3"] = ln_stats_chain(tmpas[tt % 3], (ve, p5s[tt]["x2"]),
                                        bank=tt % 3, presum=p5s[tt]["x2"])

    def p5_x3(tt):
        par = tt % 3
        tmpa, x3b = tmpas[par], x3bs[tt % 2]
        v_x3 = ln_normalize(x3b[:, :], tmpa[:, :], p5s[tt]["ln3"],
                            extra_waits=[(mm, x3b_free_mm[tt % 2])])
        tmpa_free[par] = [(ve, v_x3), (ls, x2_spill[tt])]
        xol_free[par] = (p5s[tt]["ln3"][0], v_x3)
        for g in range(4):
            bank = pt[g % 2]
            mmlast = None
            for j in range(4):
                dc = g * 4 + j
                w = []
                if j == 0:
                    w = [(ve, v_x3), evac_val5[g % 2]]
                    if tt == 0 and g < 2:
                        w.append((ve, pt_norm[g]))
                mmlast = p.em("tensor", lambda e, bank=bank, x3b=x3b, j=j, dc=dc: e.matmul(
                    bank[:, j * 128:(j + 1) * 128], x3b[:, dc * 128:(dc + 1) * 128],
                    sid[:, :], start=True, stop=True), waits=w, inc=mm)
            dst = bass.AP(x3T, (g * 4) * TO + tt * 128, [[16 * TO, 128], [TO, 4], [1, 128]])
            if g % 2 == 0:
                evac_val5[0] = (ve, p.em("vector", lambda e, bank=bank, dst=dst:
                                         e.tensor_copy(dst, bank[:, :]),
                                         waits=[(mm, mmlast)], inc=ve))
            else:
                evac_val5[1] = (ac, p.em("scalar", lambda e, bank=bank, dst=dst:
                                         e.copy(dst, bank[:, :]),
                                         waits=[(mm, mmlast)], inc=ac))
        x3b_free_mm[tt % 2] = mm.n
        if tt == 3:
            x3T_half_w[0] = [(ve, ve.n), (ac, ac.n)]

    pipeline(8, [p5_load, p5_attio, p5_ln2, p5_x2, p5_ln3, p5_x3], skew=2)
    x3T_half_w[1] = [(ve, ve.n), (ac, ac.n)]

    x3T_ready_w = [(ve, ve.n), (ac, ac.n)]
    scopeD.close()

    scopeE1 = contextlib.ExitStack()
    E1 = scopeE1.enter_context
    wgs = [E1(nc.sbuf_tensor(f"wgs{i}", [128, 2048], BF16)) for i in range(2)]
    wus = [E1(nc.sbuf_tensor(f"wus{i}", [128, 2048], BF16)) for i in range(2)]
    wds = [E1(nc.sbuf_tensor(f"wds{i}", [128, D], BF16)) for i in range(NWD)]
    gu = E1(nc.sbuf_tensor("gu", [128, HPB * TO], BF16))
    gtmp = E1(nc.sbuf_tensor("gtmp", [128, 1024], BF16))
    sxr = E1(nc.sbuf_tensor("sxr", [1, TO], F32))
    sxt = E1(nc.sbuf_tensor("sxt", [1, TO], F32))
    sxlo1 = E1(nc.sbuf_tensor("sxlo1", [1, TO], BF16))
    sxh = E1(nc.sbuf_tensor("sxh", [2, TO], BF16))
    ones2 = E1(nc.sbuf_tensor("ones2", [2, 512], BF16))
    sgur, sgur2, sguh = sxr, sxt, sxh

    p.em("gpsimd", lambda e: e.memset(ones2[:, :], 1.0),
         waits=list(x3T_ready_w), inc=lg)
    misc6_ld = lg.n

    sxmm = [0, 0]
    for half in range(2):
        for dc in range(16):
            sxmm[half] = p.em("tensor", lambda e, dc=dc, half=half: e.matmul(
                qbank(half), sones128[:, :],
                x3T[:, dc * TO + half * 512: dc * TO + (half + 1) * 512],
                start=(dc == 0), stop=(dc == 15)),
                waits=list(x3T_ready_w) if (dc == 0) else [], inc=mm)
    a_sx0 = p.em("scalar", lambda e: e.copy(sxr[0:1, 0:512], qbank(0)[0:1, :]),
                 waits=[(mm, sxmm[0])], inc=ac)
    a_sx1 = p.em("scalar", lambda e: e.copy(sxr[0:1, 512:1024], qbank(1)[0:1, :]),
                 waits=[(mm, sxmm[1])], inc=ac)

    def split_hilo_plan(src, scaled, hi_lo, lo1, scale, wait0):
        s1 = p.em("vector", lambda e: e.tensor_scalar_mul(
            scaled[0:1, :], src[0:1, :], scale), waits=wait0, inc=ve)
        s2 = p.em("vector", lambda e: e.tensor_copy(hi_lo[0:1, :], scaled[0:1, :]),
                  waits=[(ve, s1)], inc=ve)
        s3 = p.em("vector", lambda e: e.tensor_sub(
            scaled[0:1, :], scaled[0:1, :], hi_lo[0:1, :]), waits=[(ve, s2)], inc=ve)
        return p.em("vector", lambda e: e.tensor_copy(lo1[0:1, :], scaled[0:1, :]),
                    waits=[(ve, s3)], inc=ve)
    v_sx0 = split_hilo_plan(sxr, sxt, sxh, sxlo1, -128.0,
                            [(ac, a_sx0), (ac, a_sx1), (lg, misc6_ld)])
    v_sx = p.em("gpsimd", lambda e: e.dma_start(out=sxh[1:2, :], in_=sxlo1[0:1, :]),
                waits=[(ve, v_sx0)], inc=lg)
    sxh_ld = lg.n

    guval = [0] * HPB
    gu_read_mm = 0
    gtmp_free_ve = [0, 0]
    a_gate_prev = [a_sx0, a_sx1]
    up_prev_ac = [0, 0]
    wg_ld = {}
    wd_ld = {}
    wbuf_free = {"g": [0, 0], "u": [0, 0]}
    wd_slot_free = [0] * NWD
    last_down_evac_ve = 0
    v_sgu_prev = 0
    down_evac_count = 0

    p5_done_ls = ls.n
    p5_done_lg = lg.n

    def emit_wgu_load(hc):
        for tag, wsrc, dst in (("g", wg, wgs), ("u", wu, wus)):
            w = [(mm, wbuf_free[tag][hc % 2])]
            if hc < 2:
                w.extend(x3T_ready_w)
                w.append((ls, p5_done_ls))
                w.append((lg, p5_done_lg))
            wg_ld[(tag, hc)] = p.em("gpsimd", lambda e, wsrc=wsrc, hc=hc, buf=dst[hc % 2]:
                                    e.dma_start(out=buf[:, :], in_=bass.AP(
                                        wsrc, hc * 16 * 128 * 128,
                                        [[128, 128], [128 * 128, 16], [1, 128]])),
                                    waits=w, inc=lw)

    def emit_wd_load(hc):
        slot = hc % NWD
        w = [(mm, wd_slot_free[slot])]
        if hc < NWD:
            w.extend(x3T_ready_w)
            w.append((ls, p5_done_ls))
            w.append((lg, p5_done_lg))
        wd_ld[hc] = p.em("gpsimd", lambda e, hc=hc, buf=wds[slot]: e.dma_start(
            out=buf[:, :], in_=bass.AP(wd, hc * 128 * D, [[D, 128], [1, D]])),
            waits=w, inc=lw)

    emit_wgu_load(0)
    emit_wgu_load(1)
    emit_wd_load(0)

    x3f = x3T[:, :].bitcast(F32)
    tmpa7s = [x3f[:, 0:2048], x3f[:, 2048:4096]]
    tmpb7 = x3f[:, 4096:6144]
    ssdb = x3f[:, 6144:8192]
    p7state = {"sdb_ld": 0, "v_sgut": 0, "tmpa7_free": [(ac, 0), (ac, 0)],
               "tmpb7_free": 0, "fin_evac": [0] * 8}
    store_vals = []
    p7s = [dict() for _ in range(8)]

    def p7_a(tc):
        par = tc % 2
        tmpa7 = tmpa7s[par]
        p7s[tc]["fin"] = p.em("vector", lambda e, tc=tc, tmpa7=tmpa7, par=par:
                              e.scalar_tensor_tensor(
                                  tmpa7, down[:, tc * D:(tc + 1) * D], sguT[:, tc:tc + 1],
                                  ssdb, AluOpType.subtract, AluOpType.mult,
                                  accum_out=stx[:, par * 8:par * 8 + 1]),
            waits=[(ve, p7state["v_sgut"]), (ve, p7state["fin_evac"][tc]),
                   p7state["tmpa7_free"][par], (lg, p7state["sdb_ld"])], inc=ve)

    def p7_b(tc):
        p7s[tc]["ln4"] = ln_stats_chain(tmpa7s[tc % 2], (ve, p7s[tc]["fin"]),
                                        bank=tc % 2, presum=p7s[tc]["fin"])

    def p7_c(tc):
        par = tc % 2
        tmpa7 = tmpa7s[par]
        v_ln4 = ln_normalize(tmpb7, tmpa7, p7s[tc]["ln4"],
                             extra_waits=[(ve, p7state["tmpb7_free"])])
        x2ld = p.em("sync", lambda e, tc=tc, tmpa7=tmpa7: e.dma_start(
            out=tmpa7, in_=bass.AP(x2d, tc * 128 * D, [[D, 128], [1, D]])),
            waits=[(ls, x2_spill[tc]), (ve, v_ln4)], inc=ls)
        v_out = p.em("vector", lambda e, tmpa7=tmpa7: e.tensor_add(
            tmpa7, tmpa7, tmpb7), waits=[(ve, v_ln4), (ls, x2ld)], inc=ve)
        p7state["tmpb7_free"] = v_out
        sv = p.em("sync", lambda e, tc=tc, tmpa7=tmpa7: e.dma_start(
            out=bass.AP(out, tc * 128 * D, [[D, 128], [1, D]]), in_=tmpa7),
            waits=[(ve, v_out)], inc=ls)
        p7state["tmpa7_free"][par] = (ls, sv)
        p7s[tc]["store"] = sv
        store_vals.append(sv)

    for blk in range(NB):
        for k in range(HPB):
            hc = blk * HPB + k
            mmg = mmu = 0
            for qh in range(2):
                for dc in range(16):
                    w = []
                    if dc == 0:
                        w = [(lw, wg_ld[("g", hc)]), (ac, a_gate_prev[qh]), (lg, sxh_ld)]
                    p.em("tensor", lambda e, hc=hc, dc=dc, qh=qh: e.matmul(
                        qbank(qh), wgs[hc % 2][:, dc * 128:(dc + 1) * 128],
                        x3T[:, dc * TO + qh * 512: dc * TO + (qh + 1) * 512],
                        start=(dc == 0), stop=False), waits=w, inc=mm)
                mmg = p.em("tensor", lambda e, qh=qh: e.matmul(
                    qbank(qh), ones2[:, 0:128], sxh[:, qh * 512:(qh + 1) * 512],
                    start=False, stop=True), inc=mm)
                a_gate = p.em("scalar", lambda e, hc=hc, qh=qh: e.activation(
                    gtmp[:, qh * 512:(qh + 1) * 512], qbank(qh), AF.Silu,
                    scale=ssg[:, hc:hc + 1]),
                    waits=[(mm, mmg), (ve, gtmp_free_ve[qh])], inc=ac)
                a_gate_prev[qh] = a_gate
                for dc in range(16):
                    w = []
                    if dc == 0:
                        w = [(lw, wg_ld[("u", hc)]), (ve, guval[k]), (mm, gu_read_mm)]
                        if up_prev_ac[qh]:
                            w.append((ac, up_prev_ac[qh]))
                    p.em("tensor", lambda e, hc=hc, dc=dc, qh=qh: e.matmul(
                        qbank(2 + qh), wus[hc % 2][:, dc * 128:(dc + 1) * 128],
                        x3T[:, dc * TO + qh * 512: dc * TO + (qh + 1) * 512],
                        start=(dc == 0), stop=False), waits=w, inc=mm)
                mmu = p.em("tensor", lambda e, qh=qh: e.matmul(
                    qbank(2 + qh), ones2[:, 0:128], sxh[:, qh * 512:(qh + 1) * 512],
                    start=False, stop=True), inc=mm)
                up_prev_ac[qh] = 0
                gv = p.em("vector", lambda e, hc=hc, k=k, qh=qh: e.scalar_tensor_tensor(
                    gu[:, k * TO + qh * 512: k * TO + (qh + 1) * 512],
                    qbank(2 + qh), ssu[:, hc:hc + 1],
                    gtmp[:, qh * 512:(qh + 1) * 512],
                    AluOpType.mult, AluOpType.mult),
                    waits=[(mm, mmu), (ac, a_gate)], inc=ve)
                guval[k] = gv
                gtmp_free_ve[qh] = gv
            wbuf_free["g"][hc % 2] = mmg
            wbuf_free["u"][hc % 2] = mmu
            if hc + 2 < HC:
                emit_wgu_load(hc + 2)
            if k + 1 < HPB:
                emit_wd_load(hc + 1)
        smm = 0
        for k in range(HPB):
            for qh in range(2):
                w = [(ve, guval[k])]
                if k == 0 and qh == 0:
                    w.append((ve, v_sgu_prev))
                smm = p.em("tensor", lambda e, k=k, qh=qh: e.matmul(
                    [pav, psm][qh][:, :], sones128[:, :],
                    gu[:, k * TO + qh * 512: k * TO + (qh + 1) * 512],
                    start=(k == 0), stop=(k == HPB - 1)), waits=w, inc=mm)
        if blk == 0:
            v_sgu = p.em("vector", lambda e: e.tensor_copy(sgur[0:1, 0:512], pav[0:1, :]),
                         waits=[(mm, smm)], inc=ve)
            v_sgu = p.em("vector", lambda e: e.tensor_copy(sgur[0:1, 512:1024], psm[0:1, :]),
                         inc=ve)
        else:
            v_sgu = p.em("vector", lambda e: e.tensor_tensor(
                sgur[0:1, 0:512], sgur[0:1, 0:512], pav[0:1, :], AluOpType.add),
                waits=[(mm, smm)], inc=ve)
            v_sgu = p.em("vector", lambda e: e.tensor_tensor(
                sgur[0:1, 512:1024], sgur[0:1, 512:1024], psm[0:1, :], AluOpType.add),
                inc=ve)
        v_sgu_prev = v_sgu
        if blk == NB - 1:
            v_s1v = split_hilo_plan(sgur, sgur2, sguh, sxlo1, 128.0, [(ve, v_sgu)])
            v_s1d = p.em("gpsimd", lambda e: e.dma_start(out=sguh[1:2, :], in_=sxlo1[0:1, :]),
                         waits=[(ve, v_s1v)], inc=lg)
            for tcb in range(8):
                w = [(ve, v_s1v), (lg, v_s1d)] if tcb == 0 else []
                if tcb == 0:
                    w.append((ac, a_gate_prev[0]))
                    w.append((ac, a_gate_prev[1]))
                p.em("tensor", lambda e, tcb=tcb: e.matmul(
                    pA[:, tcb * 2:(tcb + 1) * 2], sguh[0:2, tcb * 128:(tcb + 1) * 128],
                    sid[0:2, 0:2], start=True, stop=True), waits=w, inc=mm)
            mml = mm.n
            a_sgt = p.em("scalar", lambda e: e.copy(stx[:, 0:16], pA[:, 0:16]),
                         waits=[(mm, mml)], inc=ac)
            p7state["v_sgut"] = p.em("vector", lambda e: e.tensor_tensor(
                bass.AP(sguT, 0, [[8, 128], [1, 8], [1, 1]]),
                bass.AP(stx, 0, [[24, 128], [2, 8], [1, 1]]),
                bass.AP(stx, 1, [[24, 128], [2, 8], [1, 1]]),
                AluOpType.add), waits=[(ac, a_sgt)], inc=ve)
            p7state["sdb_ld"] = p.em("gpsimd", lambda e: e.dma_start(
                out=ssdb, in_=sdb.ap()), waits=[(ve, p7state["v_sgut"])], inc=lg)
        dbank = [pt[0], pt[1], pav, psm]
        for tc in range(8):
            mmd = [0] * 4
            for k in range(HPB):
                hc = blk * HPB + k
                for dh in range(4):
                    w = []
                    if dh == 0:
                        w.append((lw, wd_ld[hc]))
                    if k == 0:
                        w.append((ve, last_down_evac_ve))
                        if dh >= 2:
                            w.append((ve, v_sgu))
                        if blk == 0 and tc == 0 and dh < 2:
                            w.append(evac_val5[dh])
                    mmd[dh] = p.em("tensor", lambda e, k=k, tc=tc, dh=dh, b=dbank[dh]:
                                   e.matmul(b[:, :],
                                            gu[:, k * TO + tc * 128: k * TO + (tc + 1) * 128],
                                            wds[(blk * HPB + k) % NWD][:, dh * 512:(dh + 1) * 512],
                                            start=(k == 0), stop=(k == HPB - 1)),
                                   waits=w, inc=mm)
            for dh in range(4):
                if blk == 0:
                    last_down_evac_ve = p.em("vector", lambda e, tc=tc, dh=dh, b=dbank[dh]:
                                             e.tensor_copy(
                                                 down[:, tc * D + dh * 512: tc * D + (dh + 1) * 512],
                                                 b[:, :]),
                                             waits=[(mm, mmd[dh])], inc=ve)
                else:
                    last_down_evac_ve = p.em("vector", lambda e, tc=tc, dh=dh, b=dbank[dh]:
                                             e.tensor_tensor(
                                                 down[:, tc * D + dh * 512: tc * D + (dh + 1) * 512],
                                                 down[:, tc * D + dh * 512: tc * D + (dh + 1) * 512],
                                                 b[:, :], AluOpType.add),
                                             waits=[(mm, mmd[dh])], inc=ve)
            gu_read_mm = mmd[3]
            for k in range(HPB):
                wd_slot_free[(blk * HPB + k) % NWD] = mmd[3]
            if blk == NB - 1:
                p7state["fin_evac"][tc] = last_down_evac_ve
                if tc >= 1:
                    p7_b(tc - 1)
                if tc >= 2:
                    p7_c(tc - 2)
                p7_a(tc)
        if (blk + 1) < NB:
            emit_wd_load((blk + 1) * HPB)

    ffn_done_ve = ve.n
    p7_b(7)
    p7_c(6)
    p7_c(7)
    tap("x3T", x3T[:, :], [128, 16 * TO], BF16, list(x3T_ready_w))
    tap("down", down[:, :], [128, 8 * D], F32, [(ve, ffn_done_ve)])
    tap("sguT", sguT[:, :], [128, 8], F32, [(ve, p7state["v_sgut"])])
    tap("ssdb", ssdb, [128, D], F32, [(lg, p7state["sdb_ld"])])
    tap("fin0", tmpa7s[0], [128, D], F32, [(ve, p7s[0]["fin"])])

    p.em("sync", lambda e: e.nop(), waits=[(ls, store_vals[-1]), (lg, lg.n)])
    p.flush(final=True)
    nc._dbg_tensors = dbg_tensors
    return nc


_NC_CACHE = None


def _get_nc():
    global _NC_CACHE
    if _NC_CACHE is None:
        _NC_CACHE = build_nc()
    return _NC_CACHE


def kernel(L0_attn_x, L0_attn_Wq, L0_attn_bq, L0_attn_Wk, L0_attn_bk,
           L0_attn_Wv, L0_attn_bv, L0_ffn_wq_gate, L0_ffn_s_gate,
           L0_ffn_wq_up, L0_ffn_s_up, L0_ffn_wq_down, L0_ffn_s_down):
    x = np.ascontiguousarray(np.asarray(L0_attn_x, dtype=np.float32))
    Wq = np.asarray(L0_attn_Wq, dtype=np.float32)
    Wk = np.asarray(L0_attn_Wk, dtype=np.float32)
    Wv = np.asarray(L0_attn_Wv, dtype=np.float32)
    bqv = np.asarray(L0_attn_bq, dtype=np.float32)
    bkv = np.asarray(L0_attn_bk, dtype=np.float32)
    bvv = np.asarray(L0_attn_bv, dtype=np.float32)
    wqg = np.asarray(L0_ffn_wq_gate, dtype=np.int32)
    wqu = np.asarray(L0_ffn_wq_up, dtype=np.int32)
    wqd = np.asarray(L0_ffn_wq_down, dtype=np.int32)
    sgv = np.asarray(L0_ffn_s_gate, dtype=np.float32)
    suv = np.asarray(L0_ffn_s_up, dtype=np.float32)
    sdv = np.asarray(L0_ffn_s_down, dtype=np.float32)

    def tile_qk(w):
        return np.ascontiguousarray(
            w.reshape(16, 128, 8, 128).transpose(2, 0, 1, 3))

    def tile_gu(wT):
        return np.ascontiguousarray(
            wT.reshape(16, 128, HC, 128).transpose(2, 0, 1, 3))

    wgT = np.ascontiguousarray(wqg.T)
    wuT = np.ascontiguousarray(wqu.T)
    wdT = np.ascontiguousarray(wqd.T)

    ident = np.eye(128, dtype=BF)
    sg_t = np.ascontiguousarray(sgv.reshape(HC, 128).T)
    su_t = np.ascontiguousarray(suv.reshape(HC, 128).T)

    perms = {hg: np.r_[hg * 1024:2048, 0:hg * 1024] for hg in range(2)}
    wg_t, wu_t, wd_t, sdb_t = {}, {}, {}, {}
    for hg in range(2):
        pm = perms[hg]
        wg_t[hg] = np.ascontiguousarray(tile_gu(wgT[pm]).astype(BF))
        wu_t[hg] = np.ascontiguousarray(tile_gu(wuT[pm]).astype(BF))
        wd_t[hg] = np.ascontiguousarray(wdT[:, pm].reshape(HC, 128, D).astype(BF))
        sdb_t[hg] = np.ascontiguousarray(np.broadcast_to(sdv[pm][None, :], (128, D)))

    in_maps = []
    for c in range(8):
        b, hg = c // 2, c % 2
        th = c % 2
        pm = perms[hg]
        sl = slice(hg * 1024, (hg + 1) * 1024)
        xb_c = np.ascontiguousarray(
            np.concatenate([x[b, th * TO:], x[b, :th * TO]], axis=0)[:, pm])
        in_maps.append(dict(
            xb=xb_c,
            xo=np.ascontiguousarray(xb_c[0:TO]),
            wq=tile_qk(Wq[pm][:, sl]),
            wk=tile_qk(Wk[pm][:, sl]),
            wv=np.ascontiguousarray(Wv[pm][:, sl].reshape(16, 128, 1024)),
            bq=np.ascontiguousarray(bqv[sl].reshape(8, 128).T),
            bk=np.ascontiguousarray(bkv[sl].reshape(8, 128).T),
            bvb=np.ascontiguousarray(np.broadcast_to(bvv[sl][None, :], (128, 1024))),
            wg=wg_t[hg], wu=wu_t[hg], wd=wd_t[hg],
            sg=sg_t, su=su_t, sdb=sdb_t[hg],
            ident=ident,
        ))

    nc = _get_nc()
    res = run_bass_kernel_spmd(nc, in_maps, core_ids=list(range(8)))

    outp = np.empty((4, T, D), dtype=np.float32)
    for c in range(8):
        b, hg = c // 2, c % 2
        th = c % 2
        outp[b, th * TO:(th + 1) * TO] = res.results[c]["out"][:, perms[hg]]
    return outp


# version 39
# speedup vs baseline: 1.0771x; 1.0305x over previous
import contextlib
import os

import numpy as np
import ml_dtypes
import concourse.bass as bass
import concourse.mybir as mybir
from concourse.alu_op_type import AluOpType
from concourse.bass_utils import run_bass_kernel_spmd

F32 = mybir.dt.float32
BF16 = mybir.dt.bfloat16
I32 = mybir.dt.int32
AF = mybir.ActivationFunctionType
AX = mybir.AxisListType

D = 2048
T = 2048
TO = 1024
H = 8
HID = 5632
HC = 44
NB = 4
HPB = 11
NWD = 11
SQ = 0.08838834764831845
EPS = 1e-5
BF = ml_dtypes.bfloat16


class Ctr:
    def __init__(self, h, step):
        self.h, self.step, self.n = h, step, 0

    def reserve(self):
        self.n += self.step
        return self.n


class Plan:

    def __init__(self, nc):
        self.nc = nc
        self.segments = [{k: [] for k in ("tensor", "scalar", "vector", "sync", "gpsimd")}]
        self.awaited = {}

    def em(self, eng, fn, waits=(), inc=None, inc_times=1):
        waits = tuple((c, v) for (c, v) in waits if v > 0)
        for ctr, val in waits:
            assert val <= ctr.n, f"forward wait: {val} > {ctr.n}"
            self.awaited.setdefault(id(ctr), set()).add(val)
        tgt = None
        if inc is not None:
            for _ in range(inc_times):
                tgt = inc.reserve()
        self.segments[-1][eng].append((fn, waits, inc, tgt))
        return tgt

    def flush(self, tag=None, final=False):
        skip = os.environ.get("KNOFLUSH", "").split(",")
        if final:
            self.emit_all()
        elif tag not in skip:
            self.segments.append(
                {k: [] for k in ("tensor", "scalar", "vector", "sync", "gpsimd")})

    def note_awaited(self, ctr, val):
        if val > 0:
            self.awaited.setdefault(id(ctr), set()).add(val)

    def emit_all(self):
        last_inc = {}

        waited_max = {}

        def runner_for(eng, lst):
            def runner(e, lst=lst, eng=eng):
                wm = waited_max.setdefault(eng, {})
                for fn, waits, inc, tgt in lst:
                    for ctr, val in waits:
                        if wm.get(id(ctr), -1) >= val:
                            continue
                        wm[id(ctr)] = val
                        e.wait_ge(ctr.h, val)
                    ins = fn(e)
                    if inc is None:
                        continue
                    if ins is None:
                        last_inc[id(inc)] = tgt
                        continue
                    delta = tgt - last_inc.get(id(inc), 0)
                    if (inc.step == 16 or delta >= 64
                            or tgt in self.awaited.get(id(inc), ())):
                        last_inc[id(inc)] = tgt
                        ins.then_inc(inc.h, delta)
            return runner
        for seg in self.segments:
            st = {k: v for k, v in seg.items() if v}
            if not st:
                continue
            with self.nc.Block() as blk:
                for eng, lst in st.items():
                    getattr(blk, eng)(runner_for(eng, lst))
        self.segments = [{k: [] for k in ("tensor", "scalar", "vector", "sync", "gpsimd")}]


DBG = bool(os.environ.get("KDBG"))


def build_nc():
    nc = bass.Bass("TRN2", target_bir_lowering=False)
    dbg_tensors = {}


    xb = nc.dram_tensor("xb", [T, D], F32, kind="ExternalInput")
    xo = nc.dram_tensor("xo", [TO, D], F32, kind="ExternalInput")
    wq = nc.dram_tensor("wq", [8, 16, 128, 128], F32, kind="ExternalInput")
    wk = nc.dram_tensor("wk", [8, 16, 128, 128], F32, kind="ExternalInput")
    wv = nc.dram_tensor("wv", [16, 128, 1024], F32, kind="ExternalInput")
    bq = nc.dram_tensor("bq", [128, 8], F32, kind="ExternalInput")
    bk = nc.dram_tensor("bk", [128, 8], F32, kind="ExternalInput")
    bvb = nc.dram_tensor("bvb", [128, 1024], F32, kind="ExternalInput")
    wg = nc.dram_tensor("wg", [HC, 16, 128, 128], BF16, kind="ExternalInput")
    wu = nc.dram_tensor("wu", [HC, 16, 128, 128], BF16, kind="ExternalInput")
    wd = nc.dram_tensor("wd", [HC, 128, D], BF16, kind="ExternalInput")
    sg = nc.dram_tensor("sg", [128, HC], F32, kind="ExternalInput")
    su = nc.dram_tensor("su", [128, HC], F32, kind="ExternalInput")
    sdb = nc.dram_tensor("sdb", [128, D], F32, kind="ExternalInput")
    ident = nc.dram_tensor("ident", [128, 128], BF16, kind="ExternalInput")
    out = nc.dram_tensor("out", [TO, D], F32, kind="ExternalOutput")

    cc_inb = nc.dram_tensor("cc_inb", [4, TO, 256], BF16, kind="Internal")
    cc_outb = nc.dram_tensor("cc_outb", [4, 2 * TO, 256], BF16, kind="Internal")
    x2d = nc.dram_tensor("x2d", [TO, D], F32, kind="Internal")
    atmd = nc.dram_tensor("atmd", [TO, 1024], BF16, kind="Internal")

    p = Plan(nc)

    def tap(name, sb_ap, shape, dtype, waits):
        if not DBG:
            return
        dt_ = nc.dram_tensor("dbg_" + name, shape, dtype, kind="ExternalOutput")
        dbg_tensors[name] = dt_
        p.em("sync", lambda e: e.dma_start(out=dt_.ap(), in_=sb_ap), waits=waits, inc=ls)

    ctx = contextlib.ExitStack()
    E = ctx.enter_context

    ls = Ctr(E(nc.semaphore("ls")), 16)
    lg = Ctr(E(nc.semaphore("lg")), 16)
    mm = Ctr(E(nc.semaphore("mm")), 1)
    ac = Ctr(E(nc.semaphore("ac")), 1)
    ve = Ctr(E(nc.semaphore("ve")), 1)
    cc = Ctr(E(nc.semaphore("cc")), 1)
    lw = Ctr(E(nc.semaphore("lw")), 16)
    pl = Ctr(E(nc.semaphore("pl")), 1)

    pt = [E(nc.psum_tensor(f"pt{i}", [128, 512], F32)) for i in range(2)]
    pA = E(nc.psum_tensor("pA", [128, 1024], F32))
    pB = E(nc.psum_tensor("pB", [128, 1024], F32))
    pav = E(nc.psum_tensor("pav", [128, 512], F32))
    psm = E(nc.psum_tensor("psm", [128, 512], F32))

    def qbank(qq):
        t_ = pA if qq < 2 else pB
        return t_[:, (qq % 2) * 512:(qq % 2 + 1) * 512]

    sid = E(nc.sbuf_tensor("sid", [128, 128], BF16))
    sonescol = E(nc.sbuf_tensor("sonescol", [128, 1], BF16))
    sones128 = E(nc.sbuf_tensor("sones128", [128, 128], BF16))
    sonesrow = E(nc.sbuf_tensor("sonesrow", [1, 128], BF16))
    sbq = E(nc.sbuf_tensor("sbq", [128, 8], F32))
    sbk = E(nc.sbuf_tensor("sbk", [128, 8], F32))
    sbqs = E(nc.sbuf_tensor("sbqs", [128, 8], F32))
    ssg = E(nc.sbuf_tensor("ssg", [128, HC], F32))
    ssu = E(nc.sbuf_tensor("ssu", [128, HC], F32))
    stx = E(nc.sbuf_tensor("stx", [128, 24], F32))
    epsc = E(nc.sbuf_tensor("epsc", [128, 1], F32))
    junkb = E(nc.sbuf_tensor("junkb", [128, D], BF16))
    negc = E(nc.sbuf_tensor("negc", [128, H], F32))
    maxs = E(nc.sbuf_tensor("maxs", [128, 1], F32))
    maxb = E(nc.sbuf_tensor("maxb", [128, 1], BF16))
    nc1 = E(nc.sbuf_tensor("nc1", [1, 1], F32))
    nc1b = E(nc.sbuf_tensor("nc1b", [1, 1], BF16))

    p.em("gpsimd", lambda e: e.dma_start(out=sid[:, :], in_=ident.ap()), inc=lg)
    p.em("gpsimd", lambda e: e.dma_start(out=sbq[:, :], in_=bq.ap()), inc=lg)
    p.em("gpsimd", lambda e: e.dma_start(out=sbk[:, :], in_=bk.ap()), inc=lg)
    p.em("gpsimd", lambda e: e.dma_start(out=ssg[:, :], in_=sg.ap()), inc=lg)
    p.em("gpsimd", lambda e: e.dma_start(out=ssu[:, :], in_=su.ap()), inc=lg)
    p.em("gpsimd", lambda e: e.memset(sonescol[:, :], 1.0), inc=lg)
    p.em("gpsimd", lambda e: e.memset(sones128[:, :], 1.0), inc=lg)
    p.em("gpsimd", lambda e: e.memset(sonesrow[:, :], 1.0), inc=lg)
    m_eps = p.em("gpsimd", lambda e: e.memset(epsc[:, :], EPS), inc=lg)
    const_ld = lg.n
    p.em("scalar", lambda e: e.mul(sbqs[:, :], sbq[:, :], SQ),
         waits=[(lg, const_ld)], inc=ac)

    def pipeline(n_tiles, stages, skew):
        slots = []
        for tt in range(n_tiles):
            for s in range(len(stages)):
                slots.append((tt * skew + s, -s, tt, s))
        slots.sort()
        for _, _, tt, s in slots:
            stages[s](tt)

    ln_state = {"prev_norm": [0, 0, 0]}

    def ln_stats_chain(xtile, ld_wait, bank=0, presum=None, mu_ap=None, rstd_ap=None):
        B0 = bank * 8

        def c(i):
            return stx[:, B0 + i:B0 + i + 1]
        muA = c(2) if mu_ap is None else mu_ap
        rsA = c(6) if rstd_ap is None else rstd_ap
        if presum is None:
            v_sum = p.em("vector", lambda e, x=xtile: e.reduce_sum(c(0), x[:, :], AX.X),
                         waits=[ld_wait], inc=ve)
            sum_wait = (ve, v_sum)
        else:
            sum_wait = (ve, presum)
        a_sq = p.em("scalar", lambda e, x=xtile: e.activation(
            junkb[:, :], x[:, :], AF.Square, accum_out=c(1)),
            waits=[ld_wait, (ve, ln_state["prev_norm"][bank])], inc=ac)
        sMu = p.em("scalar", lambda e: e.activation(muA, c(0), AF.Identity, scale=1.0 / D),
                   waits=[sum_wait, (ac, a_sq)], inc=ac)
        sC = p.em("scalar", lambda e: e.activation(c(4), c(0), AF.Square, scale=1.0 / D),
                  waits=[sum_wait, (ac, sMu)], inc=ac)
        sEx = p.em("scalar", lambda e: e.activation(c(3), c(1), AF.Identity,
                                                    scale=1.0 / D, bias=epsc[:, :]),
                   waits=[(ac, sC), (lg, m_eps)], inc=ac)
        sD = p.em("scalar", lambda e: e.activation(c(5), c(4), AF.Identity,
                                                   scale=-1.0, bias=c(3)),
                  waits=[(ac, sEx)], inc=ac)
        vE = p.em("vector", lambda e: e.reciprocal(c(5), c(5)),
                  waits=[(ac, sD)], inc=ve)
        a_rstd = p.em("scalar", lambda e: e.activation(rsA, c(5), AF.Sqrt),
                      waits=[(ve, vE)], inc=ac)
        return a_rstd, muA, rsA, bank

    def ln_normalize(out_ap, in_ap, stats, extra_waits=()):
        a_rstd, mu_ap, rstd_ap, bank = stats
        v_norm = p.em("vector", lambda e: e.tensor_scalar(
            out_ap, in_ap, mu_ap, rstd_ap, AluOpType.subtract, AluOpType.mult),
            waits=[(ac, a_rstd)] + list(extra_waits), inc=ve)
        ln_state["prev_norm"][bank] = v_norm
        return v_norm

    statbuf = E(nc.sbuf_tensor("statbuf", [128, 32], F32))

    scopeB2 = contextlib.ExitStack()
    EB = scopeB2.enter_context
    qT = EB(nc.sbuf_tensor("qT", [128, 8 * T], BF16))
    kT = EB(nc.sbuf_tensor("kT", [128, 8 * T], BF16))
    vsb = EB(nc.sbuf_tensor("vsb", [128, 16 * 1024], BF16))

    scopeA2 = contextlib.ExitStack()
    x1T = scopeA2.enter_context(nc.sbuf_tensor("x1T", [128, 16 * T], BF16))

    scopeA1 = contextlib.ExitStack()
    xin = [scopeA1.enter_context(nc.sbuf_tensor(f"xin{i}", [128, D], F32)) for i in range(3)]
    x1bs = [scopeA1.enter_context(nc.sbuf_tensor(f"x1b{i}", [128, D], BF16)) for i in range(3)]

    evac_val = [(ve, 0), (ac, 0)]
    xin_free = [(0, 0)] * 3
    x1b_free_mm = [0, 0, 0]
    p1s = [dict() for _ in range(16)]

    def p1_load(tt):
        buf = xin[tt % 3]
        fa, fv = xin_free[tt % 3]
        eng = "sync" if tt % 2 == 0 else "gpsimd"
        ctr = ls if tt % 2 == 0 else lg
        p1s[tt]["ld"] = (ctr, p.em(eng, lambda e, tt=tt, buf=buf: e.dma_start(
            out=buf[:, :], in_=bass.AP(xb, tt * 128 * D, [[D, 128], [1, D]])),
            waits=[(ac, fa), (ve, fv)], inc=ctr))

    def p1_stats(tt):
        buf = xin[tt % 3]
        p1s[tt]["ln"] = ln_stats_chain(
            buf, p1s[tt]["ld"], bank=tt % 3,
            mu_ap=statbuf[:, 2 * tt:2 * tt + 1], rstd_ap=statbuf[:, 2 * tt + 1:2 * tt + 2])

    def p1_norm(tt):
        buf = xin[tt % 3]
        stats = p1s[tt]["ln"]
        x1b = x1bs[tt % 3]
        v_norm = ln_normalize(x1b[:, :], buf[:, :], stats,
                              extra_waits=[(mm, x1b_free_mm[tt % 3])])
        xin_free[tt % 3] = (stats[0], v_norm)
        for g in range(4):
            bank = pt[g % 2]
            mmlast = None
            for j in range(4):
                dc = g * 4 + j
                w = []
                if j == 0:
                    w = [(ve, v_norm), evac_val[g % 2]]
                mmlast = p.em("tensor", lambda e, bank=bank, x1b=x1b, j=j, dc=dc: e.matmul(
                    bank[:, j * 128:(j + 1) * 128], x1b[:, dc * 128:(dc + 1) * 128],
                    sid[:, :], start=True, stop=True), waits=w, inc=mm)
            dst = bass.AP(x1T, (g * 4) * T + tt * 128, [[16 * T, 128], [T, 4], [1, 128]])
            eng = "vector" if g % 2 == 0 else "scalar"
            ctr = ve if g % 2 == 0 else ac
            if g % 2 == 0:
                evac_val[0] = (ve, p.em("vector", lambda e, bank=bank, dst=dst:
                                        e.tensor_copy(dst, bank[:, :]),
                                        waits=[(mm, mmlast)], inc=ve))
            else:
                evac_val[1] = (ac, p.em("scalar", lambda e, bank=bank, dst=dst:
                                        e.copy(dst, bank[:, :]),
                                        waits=[(mm, mmlast)], inc=ac))
        x1b_free_mm[tt % 3] = mm.n

    pipeline(16, [p1_load, p1_stats, p1_norm], skew=1)

    x1T_ready_ve = ve.n
    x1T_ready_ac = ac.n
    p1_done_mm = mm.n
    tap("x1T", x1T[:, :], [128, 16 * T], BF16, [(ve, ve.n)])
    p1_done_ac = ac.n
    p1_done_ve = ve.n
    scopeA1.close()

    scopeB1 = contextlib.ExitStack()
    wqs = [scopeB1.enter_context(nc.sbuf_tensor(f"wqs{i}", [128, 2048], BF16))
           for i in range(2)]

    qk_jobs = [(wq, qT, sbqs, SQ), (wk, kT, sbk, 1.0)]
    qk_evac = {}
    pq_free = [0] * 4
    slab_free = [0, 0]
    slab_ld = {}
    for i in range(2):
        j, oc = divmod(i, 8)
        slab_ld[i] = p.em("gpsimd", lambda e, w=qk_jobs[j][0], oc=oc, buf=wqs[i % 2]:
                          e.dma_start(out=buf[:, :], in_=bass.AP(
                              w, oc * 16 * 128 * 128,
                              [[128, 128], [128 * 128, 16], [1, 128]])),
                          waits=[(ac, p1_done_ac), (ve, p1_done_ve),
                                 (mm, p1_done_mm)], inc=lg)
    last_k_mm = 0
    for j, (wsrc, dstT, biasap, scale) in enumerate(qk_jobs):
        for oc in range(8):
            i = j * 8 + oc
            buf = wqs[i % 2]
            mml = [0] * 4
            for dc in range(16):
                for qq in range(4):
                    w = []
                    if dc == 0:
                        w = [(lg, slab_ld[i]), (ac, pq_free[qq])]
                        if i == 0 and qq == 0:
                            w.append((ve, x1T_ready_ve))
                            w.append((ac, x1T_ready_ac))
                    mml[qq] = p.em("tensor", lambda e, buf=buf, dc=dc, qq=qq, b=qbank(qq):
                                   e.matmul(b, buf[:, dc * 128:(dc + 1) * 128],
                                            x1T[:, dc * T + qq * 512: dc * T + (qq + 1) * 512],
                                            start=(dc == 0), stop=(dc == 15)),
                                   waits=w, inc=mm)
            last_k_mm = mml[3]
            for qq in range(4):
                a = p.em("scalar", lambda e, dstT=dstT, oc=oc, qq=qq, b=qbank(qq),
                         biasap=biasap, scale=scale: e.activation(
                             dstT[:, oc * T + qq * 512: oc * T + (qq + 1) * 512],
                             b, AF.Identity, bias=biasap[:, oc:oc + 1], scale=scale),
                         waits=[(mm, mml[qq])], inc=ac)
                pq_free[qq] = a
                qk_evac[(j, oc)] = a
            slab_free[i % 2] = ac.n
            if i + 2 < 16:
                j2, oc2 = divmod(i + 2, 8)
                slab_ld[i + 2] = p.em("gpsimd", lambda e, w=qk_jobs[j2][0], oc2=oc2,
                                      buf=wqs[i % 2]: e.dma_start(
                                          out=buf[:, :], in_=bass.AP(
                                              w, oc2 * 16 * 128 * 128,
                                              [[128, 128], [128 * 128, 16], [1, 128]])),
                                      waits=[(ac, slab_free[i % 2])], inc=lg)

    negc_ready = [0] * H
    pp = {"bank_free": [evac_val[0], evac_val[1]], "maxs_ac": 0, "maxb_mm": 0,
          "nc1_ac": 0, "nc1b_mm": 0}
    pp_state = [dict() for _ in range(H)]

    def pp0(h):
        bank = pt[h % 2]
        w = [(ac, qk_evac[(0, h)]), (ac, qk_evac[(1, h)]), pp["bank_free"][h % 2]]
        pp_state[h]["m1"] = p.em("tensor", lambda e, h=h, bank=bank: e.matmul(
            bank[:, :], kT[:, h * T: h * T + 128], qT[:, h * T: h * T + 512],
            start=True, stop=True), waits=w, inc=mm)

    def pp1(h):
        bank = pt[h % 2]
        v1 = p.em("vector", lambda e, bank=bank: e.reduce_max(maxs[:, 0:1], bank[:, :], AX.X),
                  waits=[(mm, pp_state[h]["m1"]), (ac, pp["maxs_ac"])], inc=ve)
        a1 = p.em("scalar", lambda e: e.copy(maxb[:, :], maxs[:, 0:1]),
                  waits=[(ve, v1), (mm, pp["maxb_mm"])], inc=ac)
        pp["maxs_ac"] = a1
        pp_state[h]["a1"] = a1

    def pp2(h):
        bank = pt[h % 2]
        m2 = p.em("tensor", lambda e, bank=bank: e.matmul(
            bank[0:1, 0:128], maxb[:, :], sid[:, :], start=True, stop=True),
            waits=[(ac, pp_state[h]["a1"])], inc=mm)
        pp["maxb_mm"] = m2
        v2 = p.em("vector", lambda e, bank=bank: e.reduce_max(
            nc1[0:1, 0:1], bank[0:1, 0:128], AX.X, negate=True),
            waits=[(mm, m2), (ac, pp["nc1_ac"])], inc=ve)
        a2 = p.em("scalar", lambda e: e.copy(nc1b[:, :], nc1[0:1, 0:1]),
                  waits=[(ve, v2), (mm, pp["nc1b_mm"])], inc=ac)
        pp_state[h]["a2"] = a2
        pp["nc1_ac"] = a2

    def pp3(h):
        bank = pt[h % 2]
        m3 = p.em("tensor", lambda e, bank=bank: e.matmul(
            bank[:, 300:301], sonesrow[:, :], nc1b[:, :], start=True, stop=True),
            waits=[(ac, pp_state[h]["a2"])], inc=mm)
        pp["nc1b_mm"] = m3
        a3 = p.em("scalar", lambda e, h=h, bank=bank: e.copy(negc[:, h:h + 1], bank[:, 300:301]),
                  waits=[(mm, m3)], inc=ac)
        pp["bank_free"][h % 2] = (ac, a3)
        negc_ready[h] = a3

    pp_stages = [pp0, pp1, pp2, pp3]
    pp_slots = {}
    for h in range(H):
        for k in range(4):
            pp_slots.setdefault(2 * h + k, []).append((4 - k, h, k))
    for v in pp_slots.values():
        v.sort()

    def pp_emit_slot(t):
        for _, h, k in pp_slots.get(t, []):
            pp_stages[k](h)

    scopeB1b = contextlib.ExitStack()
    wvf = scopeB1b.enter_context(nc.sbuf_tensor("wvf", [128, 16 * 1024], BF16))
    sbvb = scopeB1b.enter_context(nc.sbuf_tensor("sbvb", [128, 1024], BF16))
    p.em("gpsimd", lambda e: e.dma_start(out=sbvb[:, :], in_=bvb.ap()), inc=lg)
    for dc in range(16):
        p.em("gpsimd", lambda e, dc=dc: e.dma_start(
            out=wvf[:, dc * 1024:(dc + 1) * 1024],
            in_=bass.AP(wv, dc * 128 * 1024, [[1024, 128], [1, 1024]])), inc=lg)
    wv_ld = lg.n
    v_evac_prev = 0
    for tc in range(16):
        mml = [0, 0]
        for dc in range(16):
            for half in range(2):
                w = []
                if dc == 0:
                    w = [(lg, wv_ld), (ve, v_evac_prev), (ac, pq_free[2 + half])]
                mml[half] = p.em("tensor", lambda e, dc=dc, tc=tc, half=half, b=[pav, psm][half]:
                                 e.matmul(b[:, :],
                                          x1T[:, dc * T + tc * 128: dc * T + (tc + 1) * 128],
                                          wvf[:, dc * 1024 + half * 512: dc * 1024 + (half + 1) * 512],
                                          start=(dc == 0), stop=(dc == 15)),
                                 waits=w, inc=mm)
        for half in range(2):
            v_evac_prev = p.em("vector", lambda e, tc=tc, half=half, b=[pav, psm][half]:
                               e.tensor_tensor(
                                   vsb[:, tc * 1024 + half * 512: tc * 1024 + (half + 1) * 512],
                                   b[:, :], sbvb[:, half * 512:(half + 1) * 512],
                                   AluOpType.add),
                               waits=[(mm, mml[half])], inc=ve)
        pp_emit_slot(tc)

    tap("qT", qT[:, :], [128, 8 * T], BF16, [(ac, ac.n)])
    tap("kT", kT[:, :], [128, 8 * T], BF16, [(ac, ac.n)])
    tap("vsb", vsb[:, :], [128, 16 * 1024], BF16, [(ve, ve.n)])
    for t in range(16, 20):
        pp_emit_slot(t)
    scopeB1b.close()
    scopeB1.close()
    scopeA2.close()

    scopeC = contextlib.ExitStack()
    EC = scopeC.enter_context
    expT = EC(nc.sbuf_tensor("expT", [128, 16 * 512], BF16))
    avhs = [EC(nc.sbuf_tensor(f"avh{i}", [128, T], BF16)) for i in range(2)]
    atm_peer = EC(nc.sbuf_tensor("atm_peer", [128, 8 * 1024], BF16))
    atm_own = EC(nc.sbuf_tensor("atm_own", [128, 8 * 1024], BF16))
    sumsrows = [EC(nc.sbuf_tensor(f"sumsrow{i}", [1, T], F32)) for i in range(2)]
    sumhi = EC(nc.sbuf_tensor("sumhi", [1, T], BF16))
    sumlo32 = EC(nc.sbuf_tensor("sumlo32", [1, T], F32))
    sumlo = EC(nc.sbuf_tensor("sumlo", [1, T], BF16))
    recipTs = [EC(nc.sbuf_tensor(f"recipT{i}", [128, 16], F32)) for i in range(2)]

    PAIRS = [pA, pB]
    exp_pair_val = [pq_free[1], pq_free[3]]
    onesval = [0] * 16
    avcopy_prev = v_evac_prev
    avcopy_prev_ac = 0
    sumscopy_prev_ac = 0
    recip_guard = [pp["bank_free"][0], pp["bank_free"][1]]
    pt_norm = [0, 0]
    epi_pending = None

    epi_state = {"sums_tr_mm": 0}

    def emit_epilogue(hp, sums_ac, av_ac):
        par = hp % 2
        sumsrow = sumsrows[par]
        recipT = recipTs[par]
        avh = avhs[par]
        v1 = p.em("vector", lambda e: e.tensor_copy(sumhi[:, :], sumsrow[:, :]),
                  waits=[(ac, sums_ac), (mm, epi_state["sums_tr_mm"])], inc=ve)
        v2 = p.em("vector", lambda e: e.tensor_sub(sumlo32[:, :], sumsrow[:, :], sumhi[:, :]),
                  waits=[(ve, v1)], inc=ve)
        v3 = p.em("vector", lambda e: e.tensor_copy(sumlo[:, :], sumlo32[:, :]),
                  waits=[(ve, v2)], inc=ve)
        mml = 0
        for qt in range(16):
            w = [(ve, v3), recip_guard[0]] if qt == 0 else []
            p.em("tensor", lambda e, qt=qt: e.matmul(
                pt[0][:, qt:qt + 1], sumhi[0:1, qt * 128:(qt + 1) * 128],
                sonesrow[0:1, 0:1], start=True, stop=True), waits=w, inc=mm)
            mml = p.em("tensor", lambda e, qt=qt: e.matmul(
                pt[0][:, qt:qt + 1], sumlo[0:1, qt * 128:(qt + 1) * 128],
                sonesrow[0:1, 0:1], start=False, stop=True), inc=mm)
        epi_state["sums_tr_mm"] = mml
        rv = p.em("vector", lambda e, recipT=recipT: e.reciprocal(
            recipT[:, 0:16], pt[0][:, 0:16]), waits=[(mm, mml)], inc=ve)
        for qt in range(16):
            bank = pt[qt % 2]
            w = [(ve, pt_norm[qt % 2]), (ac, av_ac)]
            if qt == 0:
                w.append((ve, rv))
            if qt == 1:
                w.append(recip_guard[1])
            m1 = p.em("tensor", lambda e, qt=qt, bank=bank, avh=avh: e.matmul(
                bank[:, 0:128], avh[:, qt * 128:(qt + 1) * 128], sid[:, :],
                start=True, stop=True), waits=w, inc=mm)
            dst = (atm_own[:, qt * 1024 + hp * 128: qt * 1024 + (hp + 1) * 128]
                   if qt < 8 else
                   atm_peer[:, (qt - 8) * 1024 + hp * 128: (qt - 8) * 1024 + (hp + 1) * 128])
            pt_norm[qt % 2] = p.em("vector",
                                   lambda e, qt=qt, dst=dst, bank=bank, recipT=recipT:
                                   e.tensor_scalar(
                                       dst,
                                       bank[:, 0:128], recipT[:, qt:qt + 1], None,
                                       AluOpType.mult),
                                   waits=[(mm, m1)], inc=ve)
        recip_guard[1] = (ve, pt_norm[1])
        recip_guard[0] = (ve, pt_norm[0])

    attn_q_ve = [0, 0, 0, 0]
    for h in range(H):
        avh = avhs[h % 2]
        sumsrow = sumsrows[h % 2]
        for qb in range(4):
            scv = [0] * 16
            for mp in range(8):
                mj0 = 2 * mp
                P = mp % 2
                bankT = PAIRS[P]
                for k in range(2):
                    mj = mj0 + k
                    w = []
                    if k == 0:
                        w.append((ac, exp_pair_val[P]))
                        if mj0 == 0:
                            w.append((ac, negc_ready[h]))
                    scv[mj] = p.em("tensor", lambda e, h=h, qb=qb, mj=mj, k=k, bankT=bankT:
                                   e.matmul(
                                       bankT[:, k * 512:(k + 1) * 512],
                                       kT[:, h * T + mj * 128: h * T + (mj + 1) * 128],
                                       qT[:, h * T + qb * 512: h * T + (qb + 1) * 512],
                                       start=True, stop=True), waits=w, inc=mm)
                we = [(mm, scv[mj0 + 1]), (mm, onesval[mj0]), (mm, onesval[mj0 + 1])]
                if h == 0 and qb == 0 and mp == 0:
                    we.append((ve, v_evac_prev))
                exp_pair_val[P] = p.em("scalar", lambda e, mj0=mj0, bankT=bankT, h=h:
                                       e.activation(
                                           expT[:, mj0 * 512:(mj0 + 2) * 512],
                                           bankT[:, :], AF.Exp, bias=negc[:, h:h + 1]),
                                       waits=we, inc=ac)
                if mp >= 1:
                    for mj in (mj0 - 2, mj0 - 1):
                        wa = [(ac, exp_pair_val[(mp - 1) % 2])]
                        if mj == 0:
                            wa += [(ve, avcopy_prev), (ac, avcopy_prev_ac)]
                        p.em("tensor", lambda e, h=h, mj=mj: e.matmul(
                            pav[:, :], vsb[:, mj * 1024 + h * 128: mj * 1024 + (h + 1) * 128],
                            expT[:, mj * 512:(mj + 1) * 512],
                            start=(mj == 0), stop=(mj == 15)), waits=wa, inc=mm)
                        wo = []
                        if mj == 0:
                            wo = [(ac, sumscopy_prev_ac)]
                        onesval[mj] = p.em("tensor", lambda e, mj=mj: e.matmul(
                            psm[:, :], sones128[:, :], expT[:, mj * 512:(mj + 1) * 512],
                            start=(mj == 0), stop=(mj == 15)), waits=wo, inc=mm)
            for mj in (14, 15):
                wa = [(ac, exp_pair_val[1])]
                p.em("tensor", lambda e, h=h, mj=mj: e.matmul(
                    pav[:, :], vsb[:, mj * 1024 + h * 128: mj * 1024 + (h + 1) * 128],
                    expT[:, mj * 512:(mj + 1) * 512],
                    start=(mj == 0), stop=(mj == 15)), waits=wa, inc=mm)
                onesval[mj] = p.em("tensor", lambda e, mj=mj: e.matmul(
                    psm[:, :], sones128[:, :], expT[:, mj * 512:(mj + 1) * 512],
                    start=(mj == 0), stop=(mj == 15)), inc=mm)
            avcopy_prev_ac = p.em("scalar", lambda e, qb=qb, avh=avh: e.copy(
                avh[:, qb * 512:(qb + 1) * 512], pav[:, :]),
                waits=[(mm, onesval[15])], inc=ac)
            sumscopy_prev_ac = p.em("scalar", lambda e, qb=qb, sumsrow=sumsrow: e.copy(
                sumsrow[0:1, qb * 512:(qb + 1) * 512], psm[0:1, :]),
                waits=[(mm, onesval[15])], inc=ac)
            avcopy_prev = 0
            if qb == 0 and epi_pending is not None:
                hp = epi_pending[0]
                emit_epilogue(*epi_pending)
                epi_pending = None
                if hp % 2 == 1:
                    attn_q_ve[hp // 2] = ve.n
        epi_pending = (h, sumscopy_prev_ac, avcopy_prev_ac)
    emit_epilogue(*epi_pending)
    attn_q_ve[3] = ve.n

    attn_done_ve = ve.n
    tap("negc", negc[:, :], [128, H], F32, [(ac, ac.n)])
    tap("atm_own", atm_own[:, :], [128, 8 * 1024], BF16, [(ve, attn_done_ve)])
    tap("atm_peer", atm_peer[:, :], [128, 8 * 1024], BF16, [(ve, attn_done_ve)])

    cc_done = [0, 0, 0, 0]
    st_cc = 0
    for s in range(4):
        st_cc = p.em("gpsimd", lambda e, s=s: e.dma_start(
            out=bass.AP(cc_inb, s * TO * 256, [[256, 128], [128 * 256, 8], [1, 256]]),
            in_=bass.AP(atm_peer, s * 256, [[8 * 1024, 128], [1024, 8], [1, 256]])),
            waits=[(ve, attn_q_ve[s])], inc=lg)
        cc_done[s] = p.em("gpsimd", lambda e, s=s: e.collective_compute(
            "AllGather", AluOpType.bypass,
            replica_groups=[[0, 1], [2, 3], [4, 5], [6, 7]],
            ins=[bass.AP(cc_inb, s * TO * 256, [[256, TO], [1, 256]]).opt()],
            outs=[bass.AP(cc_outb, s * 2 * TO * 256, [[256, 2 * TO], [1, 256]]).opt()]),
            waits=[(lg, st_cc)], inc=cc)
        p.note_awaited(cc, cc_done[s])
    cc_send_done = st_cc
    spill_own = p.em("gpsimd", lambda e: e.dma_start(
        out=bass.AP(atmd, 0, [[1024, 128], [128 * 1024, 8], [1, 1024]]),
        in_=bass.AP(atm_own, 0, [[8 * 1024, 128], [1024, 8], [1, 1024]])),
        waits=[(ve, attn_done_ve)], inc=lg)

    p.flush("p4")
    scopeC.close()
    scopeB2.close()

    scopeE2 = contextlib.ExitStack()
    E2 = scopeE2.enter_context
    down = E2(nc.sbuf_tensor("down", [128, 8 * D], F32))
    sguT = E2(nc.sbuf_tensor("sguT", [128, 8], F32))

    sgur = sgur2 = sguh = None
    scopeX3T = contextlib.ExitStack()
    x3T = scopeX3T.enter_context(nc.sbuf_tensor("x3T", [128, 16 * TO], BF16))

    scopeD = contextlib.ExitStack()
    ED = scopeD.enter_context
    afp = [ED(nc.sbuf_tensor(f"afp{i}", [128, 1024], BF16)) for i in range(3)]
    afo = [ED(nc.sbuf_tensor(f"afo{i}", [128, 1024], BF16)) for i in range(3)]
    xol = [ED(nc.sbuf_tensor(f"xol{i}", [128, D], F32)) for i in range(3)]
    tmpas = [ED(nc.sbuf_tensor(f"tmpa{i}", [128, D], F32)) for i in range(3)]
    tmpbs = [ED(nc.sbuf_tensor(f"tmpb{i}", [128, D], F32)) for i in range(2)]
    x3bs = [ED(nc.sbuf_tensor(f"x3b{i}", [128, D], BF16)) for i in range(2)]

    rb_vals = [0] * 8
    afp_free = [0, 0, 0]
    own_ld = [0] * 8
    afo_free = [0, 0, 0]
    mrg_prev = [0, 0, 0]

    def emit_rb(tt):
        def fn(e, tt=tt):
            pid = e.partition_id()
            with e.register(f"off{tt}") as off:
                e.reg_alu(off, pid, 1, AluOpType.bitwise_and)
                e.reg_alu(off, off, 1, AluOpType.bitwise_xor)
                e.reg_mul(off, off, TO * 256)
                e.reg_add(off, off, tt * 128 * 256)
                e.dma_start(
                    out=bass.AP(afp[tt % 3], 0, [[1024, 128], [256, 4], [1, 256]]),
                    in_=bass.AP(cc_outb, off,
                                [[256, 128], [2 * TO * 256, 4], [1, 256]])
                ).then_inc(lg.h, 16)
            return None
        rb_vals[tt] = p.em("gpsimd", fn,
                           waits=[(cc, cc_done[3]), (pl, afp_free[tt % 3])],
                           inc=lg)

    xol_free = [(0, 0)] * 3
    evac_val5 = [(ve, 0), (ac, 0)]
    x3b_free_mm = [0, 0]
    tmpa_free = [[], [], []]
    tmpb_free_ve = [0, 0]
    x2_spill = [0] * 8
    x3T_half_w = [None, None]
    p5s = [dict() for _ in range(8)]

    def p5_load(tt):
        par = tt % 3
        buf = xol[par]
        fa, fv = xol_free[par]
        w = [(ac, fa), (ve, fv)]
        if tt < 2:
            w.append((ve, attn_done_ve))
            w.append((lg, cc_send_done))
        p5s[tt]["ld"] = (ls, p.em("sync", lambda e, tt=tt, buf=buf: e.dma_start(
            out=buf[:, :], in_=bass.AP(xo, tt * 128 * D, [[D, 128], [1, D]])),
            waits=w, inc=ls))
        emit_rb(tt)
        own_ld[tt] = p.em("sync", lambda e, tt=tt: e.dma_start(
            out=afo[tt % 3][:, :],
            in_=bass.AP(atmd, tt * 128 * 1024, [[1024, 128], [1, 1024]])),
            waits=[(lg, spill_own), (pl, afo_free[tt % 3])], inc=ls)

    def p5_attio(tt):
        par = tt % 3
        buf, tmpa = xol[par], tmpas[par]
        w = [p5s[tt]["ld"]] + tmpa_free[par]
        if tt < 3:
            w.append((ve, attn_done_ve))
            w.append((lg, cc_send_done))
        v_x1 = p.em("vector", lambda e, buf=buf, tmpa=tmpa, tt=tt: e.tensor_scalar(
            tmpa[:, :], buf[:, :], statbuf[:, 2 * tt:2 * tt + 1],
            statbuf[:, 2 * tt + 1:2 * tt + 2],
            AluOpType.subtract, AluOpType.mult), waits=w, inc=ve)
        a_sum_own = p.em("scalar", lambda e, tt=tt: e.activation(
            junkb[:, 0:1024], afo[tt % 3][:, :], AF.Identity,
            accum_out=stx[:, (tt % 3) * 8:(tt % 3) * 8 + 1]),
            waits=[(ls, own_ld[tt]), (ve, mrg_prev[par])], inc=ac)
        a_sum_peer = p.em("scalar", lambda e, tt=tt: e.activation(
            junkb[:, 1024:2048], afp[tt % 3][:, :], AF.Identity,
            accum_out=stx[:, (tt % 3) * 8 + 7:(tt % 3) * 8 + 8]),
            waits=[(lg, rb_vals[tt]), (ac, a_sum_own)], inc=ac)
        g_own = p.em("gpsimd", lambda e, tt=tt, tmpa=tmpa: e.tensor_tensor(
            tmpa[:, 0:1024], tmpa[:, 0:1024], afo[tt % 3][:, :], AluOpType.add),
            waits=[(ve, v_x1), (ls, own_ld[tt])], inc=pl)
        afo_free[tt % 3] = g_own
        g_peer = p.em("gpsimd", lambda e, tt=tt, tmpa=tmpa: e.tensor_tensor(
            tmpa[:, 1024:2048], tmpa[:, 1024:2048], afp[tt % 3][:, :], AluOpType.add),
            waits=[(lg, rb_vals[tt]), (pl, g_own)], inc=pl)
        afp_free[tt % 3] = g_peer
        p5s[tt]["pool"] = g_peer
        v_mrg = p.em("vector", lambda e, tt=tt: e.tensor_tensor(
            stx[:, (tt % 3) * 8:(tt % 3) * 8 + 1],
            stx[:, (tt % 3) * 8:(tt % 3) * 8 + 1],
            stx[:, (tt % 3) * 8 + 7:(tt % 3) * 8 + 8], AluOpType.add),
            waits=[(ac, a_sum_peer)], inc=ve)
        mrg_prev[par] = v_mrg
        p5s[tt]["attio"] = v_mrg

    def p5_ln2(tt):
        p5s[tt]["ln2"] = ln_stats_chain(tmpas[tt % 3], (pl, p5s[tt]["pool"]),
                                        bank=tt % 3, presum=p5s[tt]["attio"])

    def p5_x2(tt):
        par = tt % 3
        buf, tmpa, tmpb = xol[par], tmpas[par], tmpbs[tt % 2]
        v_ln2 = ln_normalize(tmpb[:, :], tmpa[:, :], p5s[tt]["ln2"],
                             extra_waits=[(ve, tmpb_free_ve[tt % 2])])
        v_x2 = p.em("vector", lambda e, buf=buf, tmpa=tmpa, tmpb=tmpb: e.scalar_tensor_tensor(
            tmpa[:, :], buf[:, :], 0.0, tmpb[:, :], AluOpType.add, AluOpType.add,
            accum_out=stx[:, par * 8:par * 8 + 1]), waits=[(ve, v_ln2)], inc=ve)
        tmpb_free_ve[tt % 2] = v_x2
        p5s[tt]["x2"] = v_x2
        x2_spill[tt] = p.em("sync", lambda e, tt=tt, tmpa=tmpa: e.dma_start(
            out=bass.AP(x2d, tt * 128 * D, [[D, 128], [1, D]]), in_=tmpa[:, :]),
            waits=[(ve, v_x2)], inc=ls)

    def p5_ln3(tt):
        p5s[tt]["ln3"] = ln_stats_chain(tmpas[tt % 3], (ve, p5s[tt]["x2"]),
                                        bank=tt % 3, presum=p5s[tt]["x2"])

    def p5_x3(tt):
        par = tt % 3
        tmpa, x3b = tmpas[par], x3bs[tt % 2]
        v_x3 = ln_normalize(x3b[:, :], tmpa[:, :], p5s[tt]["ln3"],
                            extra_waits=[(mm, x3b_free_mm[tt % 2])])
        tmpa_free[par] = [(ve, v_x3), (ls, x2_spill[tt])]
        xol_free[par] = (p5s[tt]["ln3"][0], v_x3)
        for g in range(4):
            bank = pt[g % 2]
            mmlast = None
            for j in range(4):
                dc = g * 4 + j
                w = []
                if j == 0:
                    w = [(ve, v_x3), evac_val5[g % 2]]
                    if tt == 0 and g < 2:
                        w.append((ve, pt_norm[g]))
                mmlast = p.em("tensor", lambda e, bank=bank, x3b=x3b, j=j, dc=dc: e.matmul(
                    bank[:, j * 128:(j + 1) * 128], x3b[:, dc * 128:(dc + 1) * 128],
                    sid[:, :], start=True, stop=True), waits=w, inc=mm)
            dst = bass.AP(x3T, (g * 4) * TO + tt * 128, [[16 * TO, 128], [TO, 4], [1, 128]])
            if g % 2 == 0:
                evac_val5[0] = (ve, p.em("vector", lambda e, bank=bank, dst=dst:
                                         e.tensor_copy(dst, bank[:, :]),
                                         waits=[(mm, mmlast)], inc=ve))
            else:
                evac_val5[1] = (ac, p.em("scalar", lambda e, bank=bank, dst=dst:
                                         e.copy(dst, bank[:, :]),
                                         waits=[(mm, mmlast)], inc=ac))
        x3b_free_mm[tt % 2] = mm.n
        if tt == 3:
            x3T_half_w[0] = [(ve, ve.n), (ac, ac.n)]

    pipeline(8, [p5_load, p5_attio, p5_ln2, p5_x2, p5_ln3, p5_x3], skew=2)
    x3T_half_w[1] = [(ve, ve.n), (ac, ac.n)]

    x3T_ready_w = [(ve, ve.n), (ac, ac.n)]
    scopeD.close()

    scopeE1 = contextlib.ExitStack()
    E1 = scopeE1.enter_context
    wgs = [E1(nc.sbuf_tensor(f"wgs{i}", [128, 2048], BF16)) for i in range(2)]
    wus = [E1(nc.sbuf_tensor(f"wus{i}", [128, 2048], BF16)) for i in range(2)]
    wds = [E1(nc.sbuf_tensor(f"wds{i}", [128, D], BF16)) for i in range(NWD)]
    gu = E1(nc.sbuf_tensor("gu", [128, HPB * TO], BF16))
    gtmp = E1(nc.sbuf_tensor("gtmp", [128, 1024], BF16))
    sxr = E1(nc.sbuf_tensor("sxr", [1, TO], F32))
    sxt = E1(nc.sbuf_tensor("sxt", [1, TO], F32))
    sxlo1 = E1(nc.sbuf_tensor("sxlo1", [1, TO], BF16))
    sxh = E1(nc.sbuf_tensor("sxh", [2, TO], BF16))
    ones2 = E1(nc.sbuf_tensor("ones2", [2, 512], BF16))
    sgur, sgur2, sguh = sxr, sxt, sxh

    p.em("gpsimd", lambda e: e.memset(ones2[:, :], 1.0),
         waits=list(x3T_ready_w), inc=lg)
    misc6_ld = lg.n

    sxmm = [0, 0]
    for half in range(2):
        for dc in range(16):
            sxmm[half] = p.em("tensor", lambda e, dc=dc, half=half: e.matmul(
                qbank(half), sones128[:, :],
                x3T[:, dc * TO + half * 512: dc * TO + (half + 1) * 512],
                start=(dc == 0), stop=(dc == 15)),
                waits=list(x3T_ready_w) if (dc == 0) else [], inc=mm)
    a_sx0 = p.em("scalar", lambda e: e.copy(sxr[0:1, 0:512], qbank(0)[0:1, :]),
                 waits=[(mm, sxmm[0])], inc=ac)
    a_sx1 = p.em("scalar", lambda e: e.copy(sxr[0:1, 512:1024], qbank(1)[0:1, :]),
                 waits=[(mm, sxmm[1])], inc=ac)

    def split_hilo_plan(src, scaled, hi_lo, lo1, scale, wait0):
        s1 = p.em("vector", lambda e: e.tensor_scalar_mul(
            scaled[0:1, :], src[0:1, :], scale), waits=wait0, inc=ve)
        s2 = p.em("vector", lambda e: e.tensor_copy(hi_lo[0:1, :], scaled[0:1, :]),
                  waits=[(ve, s1)], inc=ve)
        s3 = p.em("vector", lambda e: e.tensor_sub(
            scaled[0:1, :], scaled[0:1, :], hi_lo[0:1, :]), waits=[(ve, s2)], inc=ve)
        return p.em("vector", lambda e: e.tensor_copy(lo1[0:1, :], scaled[0:1, :]),
                    waits=[(ve, s3)], inc=ve)
    v_sx0 = split_hilo_plan(sxr, sxt, sxh, sxlo1, -128.0,
                            [(ac, a_sx0), (ac, a_sx1), (lg, misc6_ld)])
    v_sx = p.em("gpsimd", lambda e: e.dma_start(out=sxh[1:2, :], in_=sxlo1[0:1, :]),
                waits=[(ve, v_sx0)], inc=lg)
    sxh_ld = lg.n

    guval = [0] * HPB
    gu_read_mm = 0
    gtmp_free_ve = [0, 0]
    a_gate_prev = [a_sx0, a_sx1]
    up_prev_ac = [0, 0]
    wg_ld = {}
    wd_ld = {}
    wbuf_free = {"g": [0, 0], "u": [0, 0]}
    wd_slot_free = [0] * NWD
    last_down_evac_ve = 0
    v_sgu_prev = 0
    down_evac_count = 0

    p5_done_ls = ls.n
    p5_done_lg = lg.n

    def emit_wgu_load(hc):
        for tag, wsrc, dst in (("g", wg, wgs), ("u", wu, wus)):
            w = [(mm, wbuf_free[tag][hc % 2])]
            if hc < 2:
                w.extend(x3T_ready_w)
                w.append((ls, p5_done_ls))
                w.append((lg, p5_done_lg))
            wg_ld[(tag, hc)] = p.em("gpsimd", lambda e, wsrc=wsrc, hc=hc, buf=dst[hc % 2]:
                                    e.dma_start(out=buf[:, :], in_=bass.AP(
                                        wsrc, hc * 16 * 128 * 128,
                                        [[128, 128], [128 * 128, 16], [1, 128]])),
                                    waits=w, inc=lw)

    def emit_wd_load(hc):
        slot = hc % NWD
        w = [(mm, wd_slot_free[slot])]
        if hc < NWD:
            w.extend(x3T_ready_w)
            w.append((ls, p5_done_ls))
            w.append((lg, p5_done_lg))
        wd_ld[hc] = p.em("gpsimd", lambda e, hc=hc, buf=wds[slot]: e.dma_start(
            out=buf[:, :], in_=bass.AP(wd, hc * 128 * D, [[D, 128], [1, D]])),
            waits=w, inc=lw)

    emit_wgu_load(0)
    emit_wgu_load(1)
    emit_wd_load(0)

    x3f = x3T[:, :].bitcast(F32)
    tmpa7s = [x3f[:, 0:2048], x3f[:, 2048:4096]]
    tmpb7 = x3f[:, 4096:6144]
    ssdb = x3f[:, 6144:8192]
    p7state = {"sdb_ld": 0, "v_sgut": 0, "tmpa7_free": [(ac, 0), (ac, 0)],
               "tmpb7_free": 0, "fin_evac": [0] * 8}
    store_vals = []
    p7s = [dict() for _ in range(8)]

    def p7_a(tc):
        par = tc % 2
        tmpa7 = tmpa7s[par]
        p7s[tc]["fin"] = p.em("vector", lambda e, tc=tc, tmpa7=tmpa7, par=par:
                              e.scalar_tensor_tensor(
                                  tmpa7, down[:, tc * D:(tc + 1) * D], sguT[:, tc:tc + 1],
                                  ssdb, AluOpType.subtract, AluOpType.mult,
                                  accum_out=stx[:, par * 8:par * 8 + 1]),
            waits=[(ve, p7state["v_sgut"]), (ve, p7state["fin_evac"][tc]),
                   p7state["tmpa7_free"][par], (lg, p7state["sdb_ld"])], inc=ve)

    def p7_b(tc):
        p7s[tc]["ln4"] = ln_stats_chain(tmpa7s[tc % 2], (ve, p7s[tc]["fin"]),
                                        bank=tc % 2, presum=p7s[tc]["fin"])

    def p7_c(tc):
        par = tc % 2
        tmpa7 = tmpa7s[par]
        v_ln4 = ln_normalize(tmpb7, tmpa7, p7s[tc]["ln4"],
                             extra_waits=[(ve, p7state["tmpb7_free"])])
        x2ld = p.em("sync", lambda e, tc=tc, tmpa7=tmpa7: e.dma_start(
            out=tmpa7, in_=bass.AP(x2d, tc * 128 * D, [[D, 128], [1, D]])),
            waits=[(ls, x2_spill[tc]), (ve, v_ln4)], inc=ls)
        v_out = p.em("vector", lambda e, tmpa7=tmpa7: e.tensor_add(
            tmpa7, tmpa7, tmpb7), waits=[(ve, v_ln4), (ls, x2ld)], inc=ve)
        p7state["tmpb7_free"] = v_out
        sv = p.em("sync", lambda e, tc=tc, tmpa7=tmpa7: e.dma_start(
            out=bass.AP(out, tc * 128 * D, [[D, 128], [1, D]]), in_=tmpa7),
            waits=[(ve, v_out)], inc=ls)
        p7state["tmpa7_free"][par] = (ls, sv)
        p7s[tc]["store"] = sv
        store_vals.append(sv)

    for blk in range(NB):
        for k in range(HPB):
            hc = blk * HPB + k
            mmg = mmu = 0
            for qh in range(2):
                for dc in range(16):
                    w = []
                    if dc == 0:
                        w = [(lw, wg_ld[("g", hc)]), (ac, a_gate_prev[qh]), (lg, sxh_ld)]
                    p.em("tensor", lambda e, hc=hc, dc=dc, qh=qh: e.matmul(
                        qbank(qh), wgs[hc % 2][:, dc * 128:(dc + 1) * 128],
                        x3T[:, dc * TO + qh * 512: dc * TO + (qh + 1) * 512],
                        start=(dc == 0), stop=False), waits=w, inc=mm)
                mmg = p.em("tensor", lambda e, qh=qh: e.matmul(
                    qbank(qh), ones2[:, 0:128], sxh[:, qh * 512:(qh + 1) * 512],
                    start=False, stop=True), inc=mm)
                a_gate = p.em("scalar", lambda e, hc=hc, qh=qh: e.activation(
                    gtmp[:, qh * 512:(qh + 1) * 512], qbank(qh), AF.Silu,
                    scale=ssg[:, hc:hc + 1]),
                    waits=[(mm, mmg), (ve, gtmp_free_ve[qh])], inc=ac)
                a_gate_prev[qh] = a_gate
                for dc in range(16):
                    w = []
                    if dc == 0:
                        w = [(lw, wg_ld[("u", hc)]), (ve, guval[k]), (mm, gu_read_mm)]
                        if up_prev_ac[qh]:
                            w.append((ac, up_prev_ac[qh]))
                    p.em("tensor", lambda e, hc=hc, dc=dc, qh=qh: e.matmul(
                        qbank(2 + qh), wus[hc % 2][:, dc * 128:(dc + 1) * 128],
                        x3T[:, dc * TO + qh * 512: dc * TO + (qh + 1) * 512],
                        start=(dc == 0), stop=False), waits=w, inc=mm)
                mmu = p.em("tensor", lambda e, qh=qh: e.matmul(
                    qbank(2 + qh), ones2[:, 0:128], sxh[:, qh * 512:(qh + 1) * 512],
                    start=False, stop=True), inc=mm)
                up_prev_ac[qh] = 0
                gv = p.em("vector", lambda e, hc=hc, k=k, qh=qh: e.scalar_tensor_tensor(
                    gu[:, k * TO + qh * 512: k * TO + (qh + 1) * 512],
                    qbank(2 + qh), ssu[:, hc:hc + 1],
                    gtmp[:, qh * 512:(qh + 1) * 512],
                    AluOpType.mult, AluOpType.mult),
                    waits=[(mm, mmu), (ac, a_gate)], inc=ve)
                guval[k] = gv
                gtmp_free_ve[qh] = gv
            wbuf_free["g"][hc % 2] = mmg
            wbuf_free["u"][hc % 2] = mmu
            if hc + 2 < HC:
                emit_wgu_load(hc + 2)
            if k + 1 < HPB:
                emit_wd_load(hc + 1)
        smm = 0
        for k in range(HPB):
            for qh in range(2):
                w = [(ve, guval[k])]
                if k == 0 and qh == 0:
                    w.append((ve, v_sgu_prev))
                smm = p.em("tensor", lambda e, k=k, qh=qh: e.matmul(
                    [pav, psm][qh][:, :], sones128[:, :],
                    gu[:, k * TO + qh * 512: k * TO + (qh + 1) * 512],
                    start=(k == 0), stop=(k == HPB - 1)), waits=w, inc=mm)
        if blk == 0:
            v_sgu = p.em("vector", lambda e: e.tensor_copy(sgur[0:1, 0:512], pav[0:1, :]),
                         waits=[(mm, smm)], inc=ve)
            v_sgu = p.em("vector", lambda e: e.tensor_copy(sgur[0:1, 512:1024], psm[0:1, :]),
                         inc=ve)
        else:
            v_sgu = p.em("vector", lambda e: e.tensor_tensor(
                sgur[0:1, 0:512], sgur[0:1, 0:512], pav[0:1, :], AluOpType.add),
                waits=[(mm, smm)], inc=ve)
            v_sgu = p.em("vector", lambda e: e.tensor_tensor(
                sgur[0:1, 512:1024], sgur[0:1, 512:1024], psm[0:1, :], AluOpType.add),
                inc=ve)
        v_sgu_prev = v_sgu
        if blk == NB - 1:
            v_s1v = split_hilo_plan(sgur, sgur2, sguh, sxlo1, 128.0, [(ve, v_sgu)])
            v_s1d = p.em("gpsimd", lambda e: e.dma_start(out=sguh[1:2, :], in_=sxlo1[0:1, :]),
                         waits=[(ve, v_s1v)], inc=lg)
            for tcb in range(8):
                w = [(ve, v_s1v), (lg, v_s1d)] if tcb == 0 else []
                if tcb == 0:
                    w.append((ac, a_gate_prev[0]))
                    w.append((ac, a_gate_prev[1]))
                p.em("tensor", lambda e, tcb=tcb: e.matmul(
                    pA[:, tcb * 2:(tcb + 1) * 2], sguh[0:2, tcb * 128:(tcb + 1) * 128],
                    sid[0:2, 0:2], start=True, stop=True), waits=w, inc=mm)
            mml = mm.n
            a_sgt = p.em("scalar", lambda e: e.copy(stx[:, 0:16], pA[:, 0:16]),
                         waits=[(mm, mml)], inc=ac)
            p7state["v_sgut"] = p.em("vector", lambda e: e.tensor_tensor(
                bass.AP(sguT, 0, [[8, 128], [1, 8], [1, 1]]),
                bass.AP(stx, 0, [[24, 128], [2, 8], [1, 1]]),
                bass.AP(stx, 1, [[24, 128], [2, 8], [1, 1]]),
                AluOpType.add), waits=[(ac, a_sgt)], inc=ve)
            p7state["sdb_ld"] = p.em("gpsimd", lambda e: e.dma_start(
                out=ssdb, in_=sdb.ap()), waits=[(ve, p7state["v_sgut"])], inc=lg)
        dbank = [pt[0], pt[1], pav, psm]
        for tc in range(8):
            mmd = [0] * 4
            for k in range(HPB):
                hc = blk * HPB + k
                for dh in range(4):
                    w = []
                    if dh == 0:
                        w.append((lw, wd_ld[hc]))
                    if k == 0:
                        w.append((ve, last_down_evac_ve))
                        if dh >= 2:
                            w.append((ve, v_sgu))
                        if blk == 0 and tc == 0 and dh < 2:
                            w.append(evac_val5[dh])
                    mmd[dh] = p.em("tensor", lambda e, k=k, tc=tc, dh=dh, b=dbank[dh]:
                                   e.matmul(b[:, :],
                                            gu[:, k * TO + tc * 128: k * TO + (tc + 1) * 128],
                                            wds[(blk * HPB + k) % NWD][:, dh * 512:(dh + 1) * 512],
                                            start=(k == 0), stop=(k == HPB - 1)),
                                   waits=w, inc=mm)
            for dh in range(4):
                if blk == 0:
                    last_down_evac_ve = p.em("vector", lambda e, tc=tc, dh=dh, b=dbank[dh]:
                                             e.tensor_copy(
                                                 down[:, tc * D + dh * 512: tc * D + (dh + 1) * 512],
                                                 b[:, :]),
                                             waits=[(mm, mmd[dh])], inc=ve)
                else:
                    last_down_evac_ve = p.em("vector", lambda e, tc=tc, dh=dh, b=dbank[dh]:
                                             e.tensor_tensor(
                                                 down[:, tc * D + dh * 512: tc * D + (dh + 1) * 512],
                                                 down[:, tc * D + dh * 512: tc * D + (dh + 1) * 512],
                                                 b[:, :], AluOpType.add),
                                             waits=[(mm, mmd[dh])], inc=ve)
            gu_read_mm = mmd[3]
            for k in range(HPB):
                wd_slot_free[(blk * HPB + k) % NWD] = mmd[3]
            if blk == NB - 1:
                p7state["fin_evac"][tc] = last_down_evac_ve
                if tc >= 1:
                    p7_b(tc - 1)
                if tc >= 2:
                    p7_c(tc - 2)
                p7_a(tc)
        if (blk + 1) < NB:
            emit_wd_load((blk + 1) * HPB)

    ffn_done_ve = ve.n
    p7_b(7)
    p7_c(6)
    p7_c(7)
    tap("x3T", x3T[:, :], [128, 16 * TO], BF16, list(x3T_ready_w))
    tap("down", down[:, :], [128, 8 * D], F32, [(ve, ffn_done_ve)])
    tap("sguT", sguT[:, :], [128, 8], F32, [(ve, p7state["v_sgut"])])
    tap("ssdb", ssdb, [128, D], F32, [(lg, p7state["sdb_ld"])])
    tap("fin0", tmpa7s[0], [128, D], F32, [(ve, p7s[0]["fin"])])

    p.em("sync", lambda e: e.nop(), waits=[(ls, store_vals[-1]), (lg, lg.n)])
    p.flush(final=True)
    nc._dbg_tensors = dbg_tensors
    return nc


_NC_CACHE = None


def _get_nc():
    global _NC_CACHE
    if _NC_CACHE is None:
        _NC_CACHE = build_nc()
    return _NC_CACHE


def kernel(L0_attn_x, L0_attn_Wq, L0_attn_bq, L0_attn_Wk, L0_attn_bk,
           L0_attn_Wv, L0_attn_bv, L0_ffn_wq_gate, L0_ffn_s_gate,
           L0_ffn_wq_up, L0_ffn_s_up, L0_ffn_wq_down, L0_ffn_s_down):
    x = np.ascontiguousarray(np.asarray(L0_attn_x, dtype=np.float32))
    Wq = np.asarray(L0_attn_Wq, dtype=np.float32)
    Wk = np.asarray(L0_attn_Wk, dtype=np.float32)
    Wv = np.asarray(L0_attn_Wv, dtype=np.float32)
    bqv = np.asarray(L0_attn_bq, dtype=np.float32)
    bkv = np.asarray(L0_attn_bk, dtype=np.float32)
    bvv = np.asarray(L0_attn_bv, dtype=np.float32)
    wqg = np.asarray(L0_ffn_wq_gate, dtype=np.int32)
    wqu = np.asarray(L0_ffn_wq_up, dtype=np.int32)
    wqd = np.asarray(L0_ffn_wq_down, dtype=np.int32)
    sgv = np.asarray(L0_ffn_s_gate, dtype=np.float32)
    suv = np.asarray(L0_ffn_s_up, dtype=np.float32)
    sdv = np.asarray(L0_ffn_s_down, dtype=np.float32)

    def tile_qk(w):
        return np.ascontiguousarray(
            w.reshape(16, 128, 8, 128).transpose(2, 0, 1, 3))

    def tile_gu(wT):
        return np.ascontiguousarray(
            wT.reshape(16, 128, HC, 128).transpose(2, 0, 1, 3))

    wgT = np.ascontiguousarray(wqg.T)
    wuT = np.ascontiguousarray(wqu.T)
    wdT = np.ascontiguousarray(wqd.T)

    ident = np.eye(128, dtype=BF)
    sg_t = np.ascontiguousarray(sgv.reshape(HC, 128).T)
    su_t = np.ascontiguousarray(suv.reshape(HC, 128).T)

    perms = {hg: np.r_[hg * 1024:2048, 0:hg * 1024] for hg in range(2)}
    wg_t, wu_t, wd_t, sdb_t = {}, {}, {}, {}
    for hg in range(2):
        pm = perms[hg]
        wg_t[hg] = np.ascontiguousarray(tile_gu(wgT[pm]).astype(BF))
        wu_t[hg] = np.ascontiguousarray(tile_gu(wuT[pm]).astype(BF))
        wd_t[hg] = np.ascontiguousarray(wdT[:, pm].reshape(HC, 128, D).astype(BF))
        sdb_t[hg] = np.ascontiguousarray(np.broadcast_to(sdv[pm][None, :], (128, D)))

    in_maps = []
    for c in range(8):
        b, hg = c // 2, c % 2
        th = c % 2
        pm = perms[hg]
        sl = slice(hg * 1024, (hg + 1) * 1024)
        xb_c = np.ascontiguousarray(
            np.concatenate([x[b, th * TO:], x[b, :th * TO]], axis=0)[:, pm])
        in_maps.append(dict(
            xb=xb_c,
            xo=np.ascontiguousarray(xb_c[0:TO]),
            wq=tile_qk(Wq[pm][:, sl]),
            wk=tile_qk(Wk[pm][:, sl]),
            wv=np.ascontiguousarray(Wv[pm][:, sl].reshape(16, 128, 1024)),
            bq=np.ascontiguousarray(bqv[sl].reshape(8, 128).T),
            bk=np.ascontiguousarray(bkv[sl].reshape(8, 128).T),
            bvb=np.ascontiguousarray(np.broadcast_to(bvv[sl][None, :], (128, 1024))),
            wg=wg_t[hg], wu=wu_t[hg], wd=wd_t[hg],
            sg=sg_t, su=su_t, sdb=sdb_t[hg],
            ident=ident,
        ))

    nc = _get_nc()
    res = run_bass_kernel_spmd(nc, in_maps, core_ids=list(range(8)))

    outp = np.empty((4, T, D), dtype=np.float32)
    for c in range(8):
        b, hg = c // 2, c % 2
        th = c % 2
        outp[b, th * TO:(th + 1) * TO] = res.results[c]["out"][:, perms[hg]]
    return outp
